# version 32
# baseline (speedup 1.0000x reference)
import contextlib
import numpy as np
import ml_dtypes

import concourse.bass as bass
import concourse.mybir as mybir
from concourse.bass_utils import run_bass_kernel_spmd

F32 = mybir.dt.float32
BF16 = mybir.dt.bfloat16
FP8 = mybir.dt.float8e4
AF = mybir.ActivationFunctionType
ALU = mybir.AluOpType
AX = mybir.AxisListType

W = 128
T = 2048
TTOT = W + T
NT = TTOT // 128
NTM = T // 128
E60 = float(np.exp(np.float32(60.0)))
LN_EPS = 1e-5


class Ev:
    __slots__ = ("sem", "n")

    def __init__(self, sem):
        self.sem = sem
        self.n = 0


class Prog:
    def __init__(self):
        self.ops = {"Y": [], "G": [], "V": [], "S": [], "T": []}

    def add(self, eng, emit, waits=(), inc=None, k=1, chain=False):
        ws = list(waits)
        if chain and inc is not None:
            ws.append((inc, inc.n))
        tok = None
        if inc is not None:
            inc.n += k
            tok = (inc, inc.n)
        ws = tuple((ev, tv) for (ev, tv) in ws if ev is not None and tv > 0)
        self.ops[eng].append((ws, emit, inc, k))
        return tok

    def emit(self, name, eng):
        for waits, emit, inc, k in self.ops[name]:
            for ev, tokv in waits:
                eng.wait_ge(ev.sem, tokv)
            instr = emit(eng)
            if inc is not None and instr is not None:
                instr.then_inc(inc.sem, k)


NOTOK = (None, 0)


def build_nc():
    nc = bass.Bass("TRN2", target_bir_lowering=False, detect_race_conditions=False)

    def din(name, shape, dt=F32):
        return nc.dram_tensor(name, shape, dt, kind="ExternalInput").ap()

    x_d = din("x", [TTOT, 1024])
    wk_d = din("wk", [128, 8192])
    wv_d = din("wv", [128, 8192])
    wr_d = din("wr", [128, 8192])
    wo_d = din("wo", [128, 8192])
    wq_d = din("wq", [128, 8192])
    tm_d = din("tm", [128, 8])
    td_d = din("td", [128, 8])
    tf_d = din("tf", [128, 8])
    mask_d = din("mask", [128, 1])
    biasb_d = din("biasb", [128, 1024])
    idb_d = din("idb", [128, 128], BF16)
    idf_d = din("idf", [128, 128])
    ones_d = din("ones", [1, 128])

    out_d = nc.dram_tensor("out", [T, 1024], F32, kind="ExternalOutput").ap()
    cci = nc.dram_tensor("cci", [1, 16], F32)
    cco = nc.dram_tensor("cco", [1, 16], F32)

    es = contextlib.ExitStack()
    with es:
        AB = es.enter_context(nc.sbuf_tensor("AB", [128, 70700], BF16))
        AFR = es.enter_context(nc.sbuf_tensor("AF", [128, 13200], F32))
        A8 = es.enter_context(nc.sbuf_tensor("A8", [128, 2048], FP8))
        ONE = es.enter_context(nc.sbuf_tensor("ONE", [1, 192], F32))
        PS = es.enter_context(nc.psum_tensor("PS", [128, 4096], F32))

        o = 0
        def ab(n):
            nonlocal o
            r = o
            o += n
            return r
        wslA = ab(3072); wslB = ab(3072)
        wob = ab(8192); qwb = ab(8192)
        xmf = ab(8 * 2176)
        rwkb = ab(8 * 2048)
        pX1 = ab(2176); pX2 = ab(2176); pX3 = ab(2176)
        pY1 = ab(2048); pY2 = ab(2048); pY3 = ab(2048)
        vsb_o = ab(1024)
        idb_o = ab(128)
        vcar_o = ab(8)
        assert o <= 70700, o
        xlt0 = pX1; xlt1 = pX1 + 1024
        ut_o = pY1; wt2_o = pY1 + 1024
        t2t_o = pY2; qt_o = pY2 + 1024
        xqf_o = pY3
        x1bf_o = xmf

        o2 = 0
        def af(n):
            nonlocal o2
            r = o2
            o2 += n
            return r
        xa_o = af(1024); xb_o = af(1024); xc_o = af(1024)
        biasb_o = af(1024)
        bbuf_o = af(2052)
        bw_o = af(128)
        idf_o = af(128)
        ln2t_o = af(1024)
        yt_o = af(1024)
        c1_o = af(1024)
        et_o = af(1024)
        t4t_o = af(1024)
        bnst_o = af(12); ph1_o = af(2); ph2_o = af(2); mv2_o = af(2)
        dm_o = af(1); dm2_o = af(1); varf_o = af(1); u1_o = af(1)
        dmh_o = af(1); tmp16_o = af(16)
        meanf_o = mv2_o
        rcol_o = af(1); nmr_o = af(1)
        tmc_o = af(8); itm_o = af(8); tdc_o = af(8); tfc_o = af(8)
        dtmp_o = af(8); dcol_o = af(8); tfe_o = af(8)
        maskc_o = af(1)
        cst_o = af(8)
        m3a_o = af(16); r3a_o = af(16); amxa_o = af(16)
        r3sa_o = af(16); nm3a_o = af(16)
        wqam_o = af(8)
        amr_o = af(1); amw_o = af(1)
        scol_o = af(1); swcol_o = af(1); invcol_o = af(1)
        bnst2_o = [af(12), af(12)]; mv2s_o = [af(2), af(2)]
        u2_o = [af(1), af(1)]; rc2_o = [af(1), af(1)]; nm2_o = [af(1), af(1)]
        bnst3_o = [af(12), af(12)]; u3_o = [af(1), af(1)]
        mv3_o = af(32)
        nm3p_o = af(16)
        assert o2 <= 13200, o2
        x1ta_o = bbuf_o
        x1tb_o = bbuf_o + 1024
        outa_o = xa_o
        outb_o = xb_o

        ccsb_o = 0
        ccs_o = 16
        ract_o = 32; rw_o = 33
        am1_o = 34; rec1_o = 35; s1_o = 36
        aw1_o = 37; rec2_o = 38; sw1_o = 39
        prod_o = 40; inv1_o = 41
        amsc_o = 42
        ones_o = 64

        a8_o = 0; b8_o = 1024

        pcbv = [PS[:, cb * 512:(cb + 1) * 512].bitcast(BF16) for cb in range(8)]
        pqv = PS[:, 2048:2560].bitcast(BF16)

        sems = {}
        for s in ("LDA", "LDB", "LDC", "L1A", "L1B", "GDC", "G1A", "G1B",
                  "GOA", "GOB", "CCD", "TE", "VE", "SE", "CC", "GP"):
            sems[s] = Ev(es.enter_context(nc.semaphore(s)))
        LDA, LDB, LDC = sems["LDA"], sems["LDB"], sems["LDC"]
        L1A, L1B, GDC = sems["L1A"], sems["L1B"], sems["GDC"]
        G1A, G1B, GOA, GOB = sems["G1A"], sems["G1B"], sems["GOA"], sems["GOB"]
        CCD, TE, VE, SE, CC = sems["CCD"], sems["TE"], sems["VE"], sems["SE"], sems["CC"]
        GP = sems["GP"]

        P = Prog()

        def ln_combine():
            P.add("V", lambda v: v.bn_aggr(out=AFR[:, mv2_o:mv2_o + 2],
                                           in_=AFR[:, bnst_o:bnst_o + 12]), inc=VE, chain=True)

        def rstd_chain(extra_wait=NOTOK):
            w = [(VE, VE.n)]
            if extra_wait[0] is not None:
                w.append(extra_wait)
            P.add("S", lambda s: s.activation(out=AFR[:, u1_o:u1_o + 1], in_=AFR[:, mv2_o + 1:mv2_o + 2],
                                              func=AF.Ln, bias=AFR[:, cst_o:cst_o + 1]),
                  waits=w, inc=SE, chain=True)
            return P.add("S", lambda s: s.activation(out=AFR[:, rcol_o:rcol_o + 1],
                                                     in_=AFR[:, u1_o:u1_o + 1], func=AF.Exp,
                                                     scale=AFR[:, cst_o + 4:cst_o + 5]), inc=SE, chain=True)

        def nmr_op(tok_rstd):
            return P.add("V", lambda v: v.tensor_scalar(out=AFR[:, nmr_o:nmr_o + 1],
                                                        in0=AFR[:, meanf_o:meanf_o + 1],
                                                        scalar1=AFR[:, rcol_o:rcol_o + 1], scalar2=-1.0,
                                                        op0=ALU.mult, op1=ALU.mult),
                         waits=[tok_rstd], inc=VE, chain=True)

        for src_ap, dsto, ncols in ((tm_d, tmc_o, 8), (td_d, tdc_o, 8), (tf_d, tfc_o, 8),
                                    (mask_d, maskc_o, 1), (biasb_d, biasb_o, 1024)):
            P.add("G", lambda g, a=src_ap, b=dsto, n=ncols: g.dma_start(
                out=AFR[:, b:b + n], in_=a), inc=GDC, k=16)
        P.add("G", lambda g: g.dma_start(out=AB[:, idb_o:idb_o + 128], in_=idb_d), inc=GDC, k=16)
        P.add("G", lambda g: g.dma_start(out=ONE[0:1, ones_o:ones_o + 128], in_=ones_d), inc=GDC, k=16)
        tok_consts = (GDC, GDC.n)

        P.add("V", lambda v: v.tensor_scalar(out=AFR[:, itm_o:itm_o + 8], in0=AFR[:, tmc_o:tmc_o + 8],
                                             scalar1=-1.0, scalar2=1.0, op0=ALU.mult, op1=ALU.add),
              waits=[tok_consts], inc=VE)
        P.add("V", lambda v: v.memset(AB[:, vcar_o:vcar_o + 8], 0.0), inc=VE)
        P.add("V", lambda v: v.memset(ONE[0:1, ccsb_o:ccsb_o + 16], 0.0), inc=VE)
        for _i, _val in enumerate((LN_EPS, 1e-9, 1.0, -1.0, -0.5, 0.5)):
            P.add("V", lambda v, _i=_i, _val=_val: v.memset(AFR[:, cst_o + _i:cst_o + _i + 1], _val), inc=VE)
        tok_dconst_v = (VE, VE.n)
        P.add("S", lambda s: s.activation(out=AFR[:, dtmp_o:dtmp_o + 8], in_=AFR[:, tdc_o:tdc_o + 8],
                                          func=AF.Exp), waits=[tok_consts, tok_dconst_v], inc=SE)
        P.add("S", lambda s: s.activation(out=AFR[:, dcol_o:dcol_o + 8], in_=AFR[:, dtmp_o:dtmp_o + 8],
                                          func=AF.Exp, scale=AFR[:, cst_o + 3:cst_o + 4]), inc=SE, chain=True)
        P.add("S", lambda s: s.activation(out=AFR[:, tfe_o:tfe_o + 8], in_=AFR[:, tfc_o:tfc_o + 8],
                                          func=AF.Exp), inc=SE)

        P.add("G", lambda g: g.dma_start(out=cci[:, :], in_=ONE[0:1, ccsb_o:ccsb_o + 16]),
              waits=[tok_dconst_v], inc=CCD, k=16)
        P.add("G", lambda g: g.collective_compute(
            "AllReduce", ALU.max, replica_groups=[list(range(8))],
            ins=[cci.ap().opt()], outs=[cco.ap().opt()]),
            waits=[(CCD, 16)], inc=CC)

        wqslotF = [AB[:, qwb + ch * 2048: qwb + (ch + 1) * 2048].bitcast(F32) for ch in range(4)] \
            + [AB[:, wob + (ch - 4) * 2048: wob + (ch - 3) * 2048].bitcast(F32) for ch in range(4, 8)]
        tok_wql = [NOTOK] * 8
        tok_wqam = [NOTOK] * 8

        def wq_chunk(ch):
            tok_wql[ch] = P.add("Y", lambda y, ch=ch: y.dma_start(
                out=wqslotF[ch], in_=wq_d[:, ch * 1024:(ch + 1) * 1024]),
                waits=[tok_wqam[ch - 1] if ch else tok_dconst_v], inc=GDC, k=16)
            tok_wqam[ch] = P.add("V", lambda v, ch=ch: v.reduce_max(
                out=AFR[:, wqam_o + ch:wqam_o + ch + 1], in_=wqslotF[ch],
                axis=AX.X, apply_absolute_value=True),
                waits=[tok_wql[ch]], inc=VE)

        groups = [(0, 8), (8, 8), (16, 1)]
        tok_xload = [NOTOK] * NT
        tok_apply = [NOTOK] * NT
        tok_trans = [NOTOK] * NT
        tok_evac_g = [NOTOK] * len(groups)

        tok_rstd1 = [NOTOK] * NT
        tok_nmr1v = [NOTOK] * NT
        gi_of = {}
        for gi, (t0, glen) in enumerate(groups):
            for j in range(glen):
                gi_of[t0 + j] = (gi, j, t0, glen)

        def p1_load(i):
            xo = [xa_o, xb_o, xc_o][i % 3]
            ldev = [LDA, LDB, LDC][i % 3]
            wts = [tok_apply[i - 3]] if i >= 3 else []
            assert i < 3 or tok_apply[i - 3][0] is not None
            tok_xload[i] = P.add("Y", lambda y, i=i, xo=xo: y.dma_start(
                out=AFR[:, xo:xo + 1024], in_=x_d[i * 128:(i + 1) * 128, :]),
                waits=wts, inc=ldev, k=16)

        def p1_a(i):
            p = i % 2
            xo = [xa_o, xb_o, xc_o][i % 3]
            P.add("V", lambda v, xo=xo, p=p: v.bn_stats(out=AFR[:, bnst2_o[p]:bnst2_o[p] + 6],
                                                        in_=AFR[:, xo:xo + 512]),
                  waits=[tok_xload[i]], inc=VE)
            P.add("V", lambda v, xo=xo, p=p: v.bn_stats(out=AFR[:, bnst2_o[p] + 6:bnst2_o[p] + 12],
                                                        in_=AFR[:, xo + 512:xo + 1024]), inc=VE)
            P.add("V", lambda v, p=p: v.bn_aggr(out=AFR[:, mv2s_o[p]:mv2s_o[p] + 2],
                                                in_=AFR[:, bnst2_o[p]:bnst2_o[p] + 12]),
                  waits=[tok_rstd1[i - 2]] if i >= 2 else [], inc=VE, chain=True)
            tok_bn1v = (VE, VE.n)
            P.add("S", lambda s, p=p: s.activation(out=AFR[:, u2_o[p]:u2_o[p] + 1],
                                                   in_=AFR[:, mv2s_o[p] + 1:mv2s_o[p] + 2],
                                                   func=AF.Ln, bias=AFR[:, cst_o:cst_o + 1]),
                  waits=[tok_bn1v, tok_nmr1v[i - 2] if i >= 2 else NOTOK], inc=SE, chain=True)
            P.add("S", lambda s, p=p: s.activation(out=AFR[:, rc2_o[p]:rc2_o[p] + 1],
                                                   in_=AFR[:, u2_o[p]:u2_o[p] + 1], func=AF.Exp,
                                                   scale=AFR[:, cst_o + 4:cst_o + 5]), inc=SE, chain=True)
            tok_rstd1[i] = (SE, SE.n)

        def p1_b(i):
            p = i % 2
            xo = [xa_o, xb_o, xc_o][i % 3]
            xl = xlt0 if p == 0 else xlt1
            P.add("V", lambda v, p=p: v.tensor_scalar(out=AFR[:, nm2_o[p]:nm2_o[p] + 1],
                                                      in0=AFR[:, mv2s_o[p]:mv2s_o[p] + 1],
                                                      scalar1=AFR[:, rc2_o[p]:rc2_o[p] + 1], scalar2=-1.0,
                                                      op0=ALU.mult, op1=ALU.mult),
                  waits=[tok_rstd1[i], tok_apply[i - 2] if i >= 2 else NOTOK], inc=VE, chain=True)
            tok_nmr1v[i] = (VE, VE.n)
            wts = [tok_nmr1v[i]]
            if i >= 2:
                wts.append(tok_trans[i - 2])
            P.add("S", lambda s, xo=xo, xl=xl, p=p: s.activation(
                out=AB[:, xl:xl + 1024], in_=AFR[:, xo:xo + 1024], func=AF.Identity,
                bias=AFR[:, nm2_o[p]:nm2_o[p] + 1], scale=AFR[:, rc2_o[p]:rc2_o[p] + 1]),
                waits=wts, inc=SE, chain=True)
            tok_apply[i] = (SE, SE.n)

        def p1_c(i):
            gi, j, t0, glen = gi_of[i]
            p = i % 2
            xl = xlt0 if p == 0 else xlt1
            for cb in range(8):
                wts = [tok_apply[i]]
                if gi >= 1 and j == 0 and cb == 0:
                    wts.append(tok_evac_g[gi - 1])
                P.add("T", lambda t, xl=xl, cb=cb, j=j: t.transpose(
                    out=pcbv[cb][:, j * 128:(j + 1) * 128],
                    in_=AB[:, xl + cb * 128: xl + (cb + 1) * 128],
                    identity=AB[:, idb_o:idb_o + 128]),
                    waits=wts, inc=TE)
            tok_trans[i] = (TE, TE.n)
            if j != glen - 1:
                return
            glen128 = glen * 128
            for cb in range(8):
                xmo = xmf + cb * 2176 + t0 * 128
                wts = [tok_trans[t0 + glen - 1]]
                if cb == 0 and gi == 0:
                    wts.append(tok_dconst_v)
                P.add("V", lambda v, cb=cb, n=glen128: v.tensor_scalar(
                    out=AB[:, vsb_o:vsb_o + n], in0=pcbv[cb][:, 0:n],
                    scalar1=AFR[:, itm_o + cb:itm_o + cb + 1], scalar2=0.0,
                    op0=ALU.mult, op1=ALU.add),
                    waits=wts, inc=VE)
                P.add("V", lambda v, cb=cb, xmo=xmo, n=glen128: v.scalar_tensor_tensor(
                    out=AB[:, xmo + 1:xmo + n], in0=pcbv[cb][:, 1:n],
                    scalar=AFR[:, tmc_o + cb:tmc_o + cb + 1], in1=AB[:, vsb_o:vsb_o + n - 1],
                    op0=ALU.mult, op1=ALU.add), inc=VE)
                P.add("V", lambda v, cb=cb, xmo=xmo: v.scalar_tensor_tensor(
                    out=AB[:, xmo:xmo + 1], in0=pcbv[cb][:, 0:1],
                    scalar=AFR[:, tmc_o + cb:tmc_o + cb + 1], in1=AB[:, vcar_o + cb:vcar_o + cb + 1],
                    op0=ALU.mult, op1=ALU.add), inc=VE)
                P.add("V", lambda v, cb=cb, n=glen128: v.tensor_scalar(
                    out=AB[:, vcar_o + cb:vcar_o + cb + 1], in0=pcbv[cb][:, n - 1:n],
                    scalar1=AFR[:, itm_o + cb:itm_o + cb + 1], scalar2=0.0,
                    op0=ALU.mult, op1=ALU.add), inc=VE)
            tok_evac_g[gi] = (VE, VE.n)

        for r in range(NT + 1):
            if r == 0:
                p1_load(0)
                p1_load(1)
                p1_load(2)
            if r < NT:
                p1_a(r)
            if 2 <= r < 10:
                wq_chunk(r - 2)
            if r >= 1:
                p1_b(r - 1)
                p1_c(r - 1)
            if 1 <= r and r + 2 < NT:
                p1_load(r + 2)
        tok_xm_done = (VE, VE.n)

        P.add("V", lambda v: v.reduce_max(out=AFR[:, amw_o:amw_o + 1],
                                          in_=AFR[:, wqam_o:wqam_o + 8], axis=AX.X),
              waits=[tok_wqam[7]], inc=VE, chain=True)
        tok_amw = (VE, VE.n)
        P.add("G", lambda g: g.reduce_max(out=ONE[0:1, rw_o:rw_o + 1],
                                          in_=AFR[:, amw_o:amw_o + 1], axis=AX.XYZWC),
              waits=[tok_amw], inc=GP)
        tok_rw = (GP, GP.n)
        P.add("V", lambda v: v.tensor_single_scalar(out=ONE[0:1, aw1_o:aw1_o + 1],
                                                    in_=ONE[0:1, rw_o:rw_o + 1], scalar=1e-12,
                                                    op=ALU.max), waits=[tok_rw], inc=VE, chain=True)
        P.add("V", lambda v: v.reciprocal(out=ONE[0:1, rec2_o:rec2_o + 1], in_=ONE[0:1, aw1_o:aw1_o + 1]), inc=VE, chain=True)
        P.add("V", lambda v: v.tensor_single_scalar(out=ONE[0:1, sw1_o:sw1_o + 1],
                                                    in_=ONE[0:1, rec2_o:rec2_o + 1], scalar=448.0,
                                                    op=ALU.mult), inc=VE, chain=True)
        tok_sw1 = (VE, VE.n)
        P.add("T", lambda t: t.matmul(out=PS[:, 3073:3074], lhsT=ONE[0:1, ones_o:ones_o + 128],
                                      rhs=ONE[0:1, sw1_o:sw1_o + 1], start=True, stop=True),
              waits=[tok_sw1, tok_xm_done], inc=TE)
        tok_swbc = (TE, TE.n)
        P.add("V", lambda v: v.tensor_copy(out=AFR[:, swcol_o:swcol_o + 1], in_=PS[:, 3073:3074]),
              waits=[tok_swbc], inc=VE)
        tok_swcol = (VE, VE.n)

        tok_b8q = NOTOK
        tok_a8q = NOTOK
        tok_utq = NOTOK
        tok_qwq = NOTOK
        tok_qv_done = NOTOK

        def wq_quant_head(ch):
            nonlocal tok_b8q, tok_a8q
            prev_qwq = tok_qwq
            tok_c1q = P.add("G", lambda g, ch=ch: g.tensor_scalar(
                out=AFR[:, c1_o:c1_o + 1024], in0=wqslotF[ch],
                scalar1=AFR[:, swcol_o:swcol_o + 1], scalar2=0.0,
                op0=ALU.mult, op1=ALU.add),
                waits=[tok_wql[ch], tok_b8q, tok_swcol if ch == 0 else NOTOK], inc=GP)
            P.add("S", lambda s: s.activation(out=A8[:, b8_o:b8_o + 1024], in_=AFR[:, c1_o:c1_o + 1024],
                                              func=AF.Identity, scale=AFR[:, cst_o + 5:cst_o + 6]),
                  waits=[tok_c1q, prev_qwq], inc=SE)
            tok_b8q = (SE, SE.n)
            tok_ytq = P.add("G", lambda g: g.tensor_scalar(
                out=AFR[:, yt_o:yt_o + 1024], in0=AFR[:, c1_o:c1_o + 1024],
                scalar1=128.0, scalar2=-128.0, op0=ALU.min, op1=ALU.max),
                waits=[tok_a8q], inc=GP, chain=True)
            P.add("S", lambda s: s.activation(out=A8[:, a8_o:a8_o + 1024], in_=AFR[:, yt_o:yt_o + 1024],
                                              func=AF.Identity, scale=AFR[:, cst_o + 2:cst_o + 3]),
                  waits=[tok_ytq, prev_qwq], inc=SE)
            tok_a8q = (SE, SE.n)

        def wq_quant_tail(ch):
            nonlocal tok_qwq, tok_qv_done
            P.add("V", lambda v: v.tensor_scalar(
                out=AB[:, vsb_o:vsb_o + 1024], in0=A8[:, b8_o:b8_o + 1024],
                scalar1=2.0, scalar2=128.0, op0=ALU.mult, op1=ALU.min),
                waits=[tok_b8q], inc=VE)
            P.add("V", lambda v, ch=ch: v.tensor_single_scalar(out=AB[:, qwb + ch * 1024: qwb + (ch + 1) * 1024],
                                                               in_=AB[:, vsb_o:vsb_o + 1024], scalar=-128.0,
                                                               op=ALU.max), inc=VE, chain=True)
            P.add("V", lambda v, ch=ch: v.scalar_tensor_tensor(out=AB[:, vsb_o:vsb_o + 1024],
                                                               in0=A8[:, b8_o:b8_o + 1024], scalar=2.0,
                                                               in1=AB[:, qwb + ch * 1024: qwb + (ch + 1) * 1024],
                                                               op0=ALU.mult, op1=ALU.subtract), inc=VE, chain=True)
            P.add("V", lambda v, ch=ch: v.scalar_tensor_tensor(out=AB[:, qwb + ch * 1024: qwb + (ch + 1) * 1024],
                                                               in0=A8[:, a8_o:a8_o + 1024], scalar=1.0,
                                                               in1=AB[:, vsb_o:vsb_o + 1024], op0=ALU.mult,
                                                               op1=ALU.add),
                  waits=[tok_a8q], inc=VE, chain=True)
            tok_qwq = (VE, VE.n)
            tok_qv_done = tok_qwq

        wsl_bufs = [wslA, wslB]
        stage = [(xa_o, LDA), (xb_o, LDB), (xc_o, LDC)]
        tok_wdma = [[NOTOK] * 3 for _ in range(8)]
        tok_wcast = [NOTOK] * 8
        tok_wcast_wi = [[NOTOK] * 3 for _ in range(8)]
        tok_exp = [NOTOK] * 8
        tok_kv = [NOTOK] * 8
        tok_eneg = [NOTOK] * 8
        tok_kmm = [NOTOK] * 8
        tok_vmm = [NOTOK] * 8
        tok_rmm = [NOTOK] * 8

        def emit_wslice(db):
            wsl = wsl_bufs[db % 2]
            for wi, wd in enumerate((wk_d, wv_d, wr_d)):
                st, ldev = stage[wi]
                if db == 0:
                    wts = [tok_apply[NT - 1]]
                else:
                    wts = [tok_wcast_wi[db - 1][wi]]
                tok_wdma[db][wi] = P.add("Y", lambda y, wd=wd, st=st, db=db: y.dma_start(
                    out=AFR[:, st:st + 1024],
                    in_=wd.rearrange("p (cb d) -> p cb d", cb=8)[:, :, db * 128:(db + 1) * 128]),
                    waits=wts, inc=ldev, k=16)
                P.add("S", lambda s, st=st, wsl=wsl, wi=wi: s.activation(
                    out=AB[:, wsl + wi * 1024: wsl + (wi + 1) * 1024], in_=AFR[:, st:st + 1024],
                    func=AF.Identity),
                    waits=[tok_wdma[db][wi]], inc=SE)
                tok_wcast_wi[db][wi] = (SE, SE.n)
            tok_wcast[db] = (SE, SE.n)

        emit_wslice(0)

        tok_wo_cast = NOTOK

        def emit_wo():
            nonlocal tok_wo_cast
            tok_wo_cast_cb = [NOTOK] * 8
            for cb in range(8):
                st, ldev = (xa_o, LDA) if cb % 2 == 0 else (xb_o, LDB)
                wts = []
                if cb < 2:
                    wts.append(tok_wcast[7])
                if cb >= 2:
                    wts.append(tok_wo_cast_cb[cb - 2])
                tok_wo_dma = P.add("Y", lambda y, cb=cb, st=st: y.dma_start(
                    out=AFR[:, st:st + 1024],
                    in_=wo_d[:, cb * 1024:(cb + 1) * 1024]), waits=wts, inc=ldev, k=16)
                P.add("S", lambda s, cb=cb, st=st: s.activation(
                    out=AB[:, wob + cb * 1024: wob + (cb + 1) * 1024],
                    in_=AFR[:, st:st + 1024], func=AF.Identity),
                    waits=[tok_wo_dma, tok_qv_done if cb == 0 else NOTOK], inc=SE)
                tok_wo_cast_cb[cb] = (SE, SE.n)
            tok_wo_cast = (SE, SE.n)

        tok_sig_prev = NOTOK
        tok_rwkv_prev = NOTOK
        for db in range(8):
            wsl = wsl_bufs[db % 2]
            dcol_ap = AFR[:, dcol_o + db:dcol_o + db + 1]
            tfe_ap = AFR[:, tfe_o + db:tfe_o + db + 1]

            for cb in range(8):
                wts = []
                if cb == 0:
                    wts.append(tok_wcast[db])
                    wts.append(tok_xm_done)
                    if db > 0:
                        wts.append(tok_kv[db - 1])
                lw = wsl + 0 * 1024 + cb * 128
                xmb = xmf + cb * 2176
                for q in range(4):
                    P.add("T", lambda t, lw=lw, xmb=xmb, q=q, cb=cb: t.matmul(
                        out=PS[:, q * 512:(q + 1) * 512], lhsT=AB[:, lw:lw + 128],
                        rhs=AB[:, xmb + 128 + q * 512: xmb + 128 + (q + 1) * 512],
                        start=(cb == 0), stop=(cb == 7)),
                        waits=wts if q == 0 else (), inc=TE)
            tok_kmm_main = (TE, TE.n)
            for cb in range(8):
                wts = []
                if cb == 0 and db > 0:
                    wts.append(tok_eneg[db - 1])
                lw = wsl + 0 * 1024 + cb * 128
                xmb = xmf + cb * 2176
                P.add("T", lambda t, lw=lw, xmb=xmb, cb=cb: t.matmul(
                    out=PS[:, 2048:2176], lhsT=AB[:, lw:lw + 128], rhs=AB[:, xmb:xmb + 128],
                    start=(cb == 0), stop=(cb == 7)), waits=wts, inc=TE)
            tok_kmm[db] = (TE, TE.n)
            if db < 7:
                emit_wslice(db + 1)
            tok_exp_b = []
            for q in range(4):
                P.add("S", lambda s, q=q: s.activation(
                    out=AB[:, pX1 + 128 + q * 512:pX1 + 128 + (q + 1) * 512],
                    in_=PS[:, q * 512:(q + 1) * 512], func=AF.Exp),
                    waits=[tok_kmm_main] if q == 0 else [], inc=SE)
                tok_exp_b.append((SE, SE.n))
            P.add("S", lambda s: s.activation(out=AB[:, pX1:pX1 + 128], in_=PS[:, 2048:2176],
                                              func=AF.Exp), waits=[tok_kmm[db]], inc=SE)
            tok_exp[db] = (SE, SE.n)
            wq_quant_head(db)
            for cb in range(8):
                lw = wsl + 1 * 1024 + cb * 128
                xmb = xmf + cb * 2176
                for q in range(4):
                    P.add("T", lambda t, lw=lw, xmb=xmb, q=q, cb=cb: t.matmul(
                        out=PS[:, q * 512:(q + 1) * 512], lhsT=AB[:, lw:lw + 128],
                        rhs=AB[:, xmb + 128 + q * 512: xmb + 128 + (q + 1) * 512],
                        start=(cb == 0), stop=(cb == 7)),
                        waits=[tok_exp_b[q]] if cb == 0 else (), inc=TE)
            for cb in range(8):
                lw = wsl + 1 * 1024 + cb * 128
                xmb = xmf + cb * 2176
                P.add("T", lambda t, lw=lw, xmb=xmb, cb=cb: t.matmul(
                    out=PS[:, 2176:2304], lhsT=AB[:, lw:lw + 128], rhs=AB[:, xmb:xmb + 128],
                    start=(cb == 0), stop=(cb == 7)), inc=TE)
            tok_vmm[db] = (TE, TE.n)
            P.add("V", lambda v: v.scalar_tensor_tensor(out=AB[:, pX3:pX3 + 128],
                                                        in0=AB[:, pX1:pX1 + 128], scalar=E60,
                                                        in1=PS[:, 2176:2304],
                                                        op0=ALU.min, op1=ALU.mult),
                  waits=[tok_vmm[db], tok_exp[db], tok_rwkv_prev], inc=VE)
            P.add("V", lambda v: v.scalar_tensor_tensor(out=AB[:, pX3 + 128:pX3 + 2176],
                                                        in0=AB[:, pX1 + 128:pX1 + 2176], scalar=E60,
                                                        in1=PS[:, 0:2048],
                                                        op0=ALU.min, op1=ALU.mult), inc=VE)
            tok_kv[db] = (VE, VE.n)
            tok_min = P.add("V", lambda v: v.tensor_single_scalar(
                out=AB[:, pX2:pX2 + 2176], in_=AB[:, pX1:pX1 + 2176], scalar=E60,
                op=ALU.min),
                waits=[tok_sig_prev], inc=VE)
            for (src, dsty) in ((pX3, pY1), (pX2, pY2)):
                P.add("V", lambda v, d=dcol_ap, src=src: v.tensor_tensor_scan(
                    out=AFR[:, bw_o:bw_o + 128], data0=d.broadcast_to([128, 128]),
                    data1=AB[:, src:src + 128], initial=0.0, op0=ALU.mult, op1=ALU.add), inc=VE)
                P.add("V", lambda v: v.tensor_scalar(out=AFR[:, bbuf_o:bbuf_o + 1],
                                                     in0=AFR[:, bw_o + 127:bw_o + 128],
                                                     scalar1=AFR[:, maskc_o:maskc_o + 1], scalar2=0.0,
                                                     op0=ALU.mult, op1=ALU.add), inc=VE, chain=True)
                P.add("V", lambda v, d=dcol_ap, src=src: v.tensor_tensor_scan(
                    out=AFR[:, bbuf_o + 1:bbuf_o + 2049], data0=d.broadcast_to([128, 2048]),
                    data1=AB[:, src + 128:src + 2176], initial=AFR[:, bbuf_o:bbuf_o + 1],
                    op0=ALU.mult, op1=ALU.add), inc=VE, chain=True)
                P.add("V", lambda v, tf=tfe_ap, src=src, dsty=dsty: v.scalar_tensor_tensor(
                    out=AB[:, dsty:dsty + 2048], in0=AB[:, src + 128:src + 2176], scalar=tf,
                    in1=AFR[:, bbuf_o:bbuf_o + 2048], op0=ALU.mult, op1=ALU.add), inc=VE)
            tok_wq_v = (VE, VE.n)
            for cb in range(8):
                wts = []
                if cb == 0:
                    wts.append(tok_kv[db])
                    if db > 0:
                        wts.append(tok_eneg[db - 1])
                    else:
                        wts.append(tok_swcol)
                lw = wsl + 2 * 1024 + cb * 128
                xmb = xmf + cb * 2176
                for q in range(4):
                    P.add("T", lambda t, lw=lw, xmb=xmb, q=q, cb=cb: t.matmul(
                        out=PS[:, 2048 + q * 512:2048 + (q + 1) * 512], lhsT=AB[:, lw:lw + 128],
                        rhs=AB[:, xmb + 128 + q * 512: xmb + 128 + (q + 1) * 512],
                        start=(cb == 0), stop=(cb == 7)),
                        waits=wts if q == 0 else (), inc=TE)
            tok_rmm[db] = (TE, TE.n)
            P.add("S", lambda s: s.activation(out=AB[:, pX1:pX1 + 2048], in_=PS[:, 2048:4096],
                                              func=AF.Exp, scale=AFR[:, cst_o + 3:cst_o + 4]),
                  waits=[tok_rmm[db], tok_min], inc=SE)
            tok_eneg[db] = (SE, SE.n)
            P.add("S", lambda s: s.activation(out=AB[:, pY3:pY3 + 2048], in_=AB[:, pY2:pY2 + 2048],
                                              func=AF.Ln, bias=AFR[:, cst_o + 1:cst_o + 2]),
                  waits=[tok_wq_v, tok_rwkv_prev], inc=SE)
            P.add("S", lambda s: s.activation(out=AB[:, pY2:pY2 + 2048], in_=AB[:, pY3:pY3 + 2048],
                                              func=AF.Exp, scale=AFR[:, cst_o + 3:cst_o + 4]), inc=SE)
            tok_rcp = (SE, SE.n)
            P.add("G", lambda g: g.tensor_tensor(out=AB[:, pY3:pY3 + 2048], in0=AB[:, pY1:pY1 + 2048],
                                                 in1=AB[:, pY2:pY2 + 2048], op=ALU.mult),
                  waits=[tok_rcp], inc=GP)
            tok_ratio = (GP, GP.n)
            P.add("S", lambda s: s.activation(out=AB[:, pX2:pX2 + 2048], in_=AB[:, pX1:pX1 + 2048],
                                              func=AF.Ln, bias=AFR[:, cst_o + 2:cst_o + 3]),
                  waits=[tok_wq_v], inc=SE)
            P.add("S", lambda s: s.activation(out=AB[:, pX3:pX3 + 2048], in_=AB[:, pX2:pX2 + 2048],
                                              func=AF.Exp, scale=AFR[:, cst_o + 3:cst_o + 4]), inc=SE)
            tok_sig = (SE, SE.n)
            P.add("G", lambda g, db=db: g.tensor_tensor(
                out=AB[:, rwkb + db * 2048: rwkb + (db + 1) * 2048],
                in0=AB[:, pX3:pX3 + 2048], in1=AB[:, pY3:pY3 + 2048], op=ALU.mult),
                waits=[tok_sig], inc=GP)
            tok_rwkv_prev = (GP, GP.n)
            tok_sig_prev = tok_sig
            wq_quant_tail(db)
            if db == 7:
                emit_wo()
        tok_p2_v = tok_rwkv_prev
        tok_p2_s = (SE, SE.n)

        ln2tp = [ln2t_o, c1_o]
        yt3p = [yt_o, et_o]
        tok_xrel = [NOTOK] * NTM
        tok_womm = [NOTOK] * NTM
        tok_bn2v = [NOTOK] * NTM
        tok_rstd2 = [NOTOK] * NTM
        tok_nmr2v = [NOTOK] * NTM
        tok_ln2t = [NOTOK] * NTM
        tok_x1 = [NOTOK] * NTM
        tok_x1bf = [NOTOK] * NTM
        tok_rstd3 = [NOTOK] * NTM
        tok_p3v = [NOTOK] * NTM
        tok_amax = [NOTOK] * NTM

        def p3_stage_a(tc):
            p = tc % 2
            pb = p * 1024
            xro, ldev = (xa_o, LDA) if p == 0 else (xb_o, LDB)
            wts = [tok_wo_cast] if tc < 2 else [tok_x1[tc - 2]]
            tok_xrel[tc] = P.add("Y", lambda y, tc=tc, xro=xro: y.dma_start(
                out=AFR[:, xro:xro + 1024], in_=x_d[W + tc * 128: W + (tc + 1) * 128, :]),
                waits=wts, inc=ldev, k=16)
            for cb in range(8):
                wts = []
                if cb == 0:
                    wts.append(tok_wo_cast)
                    if tc == 0:
                        wts.append(tok_p2_v)
                    if tc >= 2:
                        wts.append(tok_ln2t[tc - 2])
                        wts.append(tok_bn2v[tc - 2])
                ro = rwkb + cb * 2048 + tc * 128
                P.add("T", lambda t, ro=ro, cb=cb, pb=pb: t.matmul(
                    out=PS[:, pb:pb + 512], lhsT=AB[:, ro:ro + 128],
                    rhs=AB[:, wob + cb * 1024: wob + cb * 1024 + 512],
                    start=(cb == 0), stop=(cb == 7)), waits=wts, inc=TE)
                P.add("T", lambda t, ro=ro, cb=cb, pb=pb: t.matmul(
                    out=PS[:, pb + 512:pb + 1024], lhsT=AB[:, ro:ro + 128],
                    rhs=AB[:, wob + cb * 1024 + 512: wob + (cb + 1) * 1024],
                    start=(cb == 0), stop=(cb == 7)), inc=TE)
            tok_womm[tc] = (TE, TE.n)
            P.add("V", lambda v, pb=pb, p=p: v.bn_stats(out=AFR[:, bnst2_o[p]:bnst2_o[p] + 6],
                                                        in_=PS[:, pb:pb + 512]),
                  waits=[tok_womm[tc]], inc=VE)
            P.add("V", lambda v, pb=pb, p=p: v.bn_stats(out=AFR[:, bnst2_o[p] + 6:bnst2_o[p] + 12],
                                                        in_=PS[:, pb + 512:pb + 1024]), inc=VE)
            P.add("V", lambda v, p=p: v.bn_aggr(out=AFR[:, mv2s_o[p]:mv2s_o[p] + 2],
                                                in_=AFR[:, bnst2_o[p]:bnst2_o[p] + 12]),
                  waits=[tok_rstd2[tc - 2]] if tc >= 2 else [], inc=VE, chain=True)
            tok_bn2v[tc] = (VE, VE.n)
            P.add("S", lambda s, p=p: s.activation(out=AFR[:, u2_o[p]:u2_o[p] + 1],
                                                   in_=AFR[:, mv2s_o[p] + 1:mv2s_o[p] + 2],
                                                   func=AF.Ln, bias=AFR[:, cst_o:cst_o + 1]),
                  waits=[tok_bn2v[tc], tok_nmr2v[tc - 2] if tc >= 2 else NOTOK], inc=SE, chain=True)
            P.add("S", lambda s, p=p: s.activation(out=AFR[:, rc2_o[p]:rc2_o[p] + 1],
                                                   in_=AFR[:, u2_o[p]:u2_o[p] + 1], func=AF.Exp,
                                                   scale=AFR[:, cst_o + 4:cst_o + 5]), inc=SE, chain=True)
            tok_rstd2[tc] = (SE, SE.n)

        def p3_stage_b(tc):
            p = tc % 2
            pb = p * 1024
            xro = xa_o if p == 0 else xb_o
            x1o = x1ta_o if p == 0 else x1tb_o
            P.add("V", lambda v, p=p: v.tensor_scalar(out=AFR[:, nm2_o[p]:nm2_o[p] + 1],
                                                      in0=AFR[:, mv2s_o[p]:mv2s_o[p] + 1],
                                                      scalar1=AFR[:, rc2_o[p]:rc2_o[p] + 1], scalar2=-1.0,
                                                      op0=ALU.mult, op1=ALU.mult),
                  waits=[tok_rstd2[tc], tok_ln2t[tc - 2] if tc >= 2 else NOTOK], inc=VE, chain=True)
            tok_nmr2v[tc] = (VE, VE.n)
            P.add("S", lambda s, pb=pb, p=p: s.activation(out=AFR[:, ln2tp[p]:ln2tp[p] + 1024],
                                                          in_=PS[:, pb:pb + 1024],
                                                          func=AF.Identity,
                                                          bias=AFR[:, nm2_o[p]:nm2_o[p] + 1],
                                                          scale=AFR[:, rc2_o[p]:rc2_o[p] + 1]),
                  waits=[tok_nmr2v[tc], tok_x1[tc - 2] if tc >= 2 else NOTOK], inc=SE, chain=True)
            tok_ln2t[tc] = (SE, SE.n)
            tok_x1[tc] = P.add("G", lambda g, xro=xro, x1o=x1o, p=p: g.tensor_tensor(
                out=AFR[:, x1o:x1o + 1024], in0=AFR[:, xro:xro + 1024],
                in1=AFR[:, ln2tp[p]:ln2tp[p] + 1024], op=ALU.add),
                waits=[tok_ln2t[tc], tok_xrel[tc],
                       tok_p3v[tc - 2] if tc >= 2 else NOTOK,
                       tok_x1bf[tc - 2] if tc >= 2 else NOTOK], inc=GP)

        def p3_stage_c(tc):
            p = tc % 2
            x1o = x1ta_o if p == 0 else x1tb_o
            P.add("S", lambda s, tc=tc, x1o=x1o: s.activation(
                out=AB[:, x1bf_o + tc * 1024: x1bf_o + (tc + 1) * 1024],
                in_=AFR[:, x1o:x1o + 1024], func=AF.Identity),
                waits=[tok_x1[tc]], inc=SE)
            tok_x1bf[tc] = (SE, SE.n)
            P.add("V", lambda v, x1o=x1o, p=p: v.bn_stats(out=AFR[:, bnst3_o[p]:bnst3_o[p] + 6],
                                                          in_=AFR[:, x1o:x1o + 512]),
                  waits=[tok_x1[tc]], inc=VE)
            P.add("V", lambda v, x1o=x1o, p=p: v.bn_stats(out=AFR[:, bnst3_o[p] + 6:bnst3_o[p] + 12],
                                                          in_=AFR[:, x1o + 512:x1o + 1024]), inc=VE)
            P.add("V", lambda v, tc=tc, p=p: v.bn_aggr(out=AFR[:, mv3_o + 2 * tc:mv3_o + 2 * tc + 2],
                                                       in_=AFR[:, bnst3_o[p]:bnst3_o[p] + 12]),
                  inc=VE, chain=True)
            tok_bn3v = (VE, VE.n)
            P.add("S", lambda s, tc=tc, p=p: s.activation(out=AFR[:, u3_o[p]:u3_o[p] + 1],
                                                          in_=AFR[:, mv3_o + 2 * tc + 1:mv3_o + 2 * tc + 2],
                                                          func=AF.Ln, bias=AFR[:, cst_o:cst_o + 1]),
                  waits=[tok_bn3v], inc=SE, chain=True)
            P.add("S", lambda s, tc=tc, p=p: s.activation(out=AFR[:, r3a_o + tc:r3a_o + tc + 1],
                                                          in_=AFR[:, u3_o[p]:u3_o[p] + 1], func=AF.Exp,
                                                          scale=AFR[:, cst_o + 4:cst_o + 5]),
                  inc=SE, chain=True)
            tok_rstd3[tc] = (SE, SE.n)

        def p3_stage_d(tc):
            p = tc % 2
            x1o = x1ta_o if p == 0 else x1tb_o
            P.add("V", lambda v, tc=tc: v.tensor_scalar(out=AFR[:, nm3p_o + tc:nm3p_o + tc + 1],
                                                        in0=AFR[:, mv3_o + 2 * tc:mv3_o + 2 * tc + 1],
                                                        scalar1=AFR[:, r3a_o + tc:r3a_o + tc + 1],
                                                        scalar2=-1.0,
                                                        op0=ALU.mult, op1=ALU.mult),
                  waits=[tok_rstd3[tc]], inc=VE, chain=True)
            P.add("V", lambda v, tc=tc, x1o=x1o, p=p: v.tensor_scalar(
                out=AFR[:, yt3p[p]:yt3p[p] + 1024], in0=AFR[:, x1o:x1o + 1024],
                scalar1=AFR[:, r3a_o + tc:r3a_o + tc + 1],
                scalar2=AFR[:, nm3p_o + tc:nm3p_o + tc + 1],
                op0=ALU.mult, op1=ALU.add),
                waits=[tok_amax[tc - 2] if tc >= 2 else NOTOK], inc=VE, chain=True)
            tok_p3v[tc] = (VE, VE.n)
            tok_amax[tc] = P.add("G", lambda g, tc=tc, p=p: g.reduce_max(
                out=ONE[0:1, amsc_o + tc:amsc_o + tc + 1], in_=AFR[:, yt3p[p]:yt3p[p] + 1024],
                axis=AX.XYZWC, apply_absolute_value=True),
                waits=[tok_p3v[tc]], inc=GP)

        for r in range(NTM + 1):
            if r < NTM:
                p3_stage_a(r)
            if r >= 1:
                p3_stage_c(r - 1)
            if r < NTM:
                p3_stage_b(r)
            if r >= 1:
                p3_stage_d(r - 1)

        P.add("G", lambda g: g.reduce_max(out=ONE[0:1, ccsb_o:ccsb_o + 1],
                                          in_=ONE[0:1, amsc_o:amsc_o + 16], axis=AX.XYZWC),
              inc=GP, chain=True)
        tok_ccsb = (GP, GP.n)
        tok_cci = P.add("G", lambda g: g.dma_start(out=cci[:, :], in_=ONE[0:1, ccsb_o:ccsb_o + 16]),
                        waits=[tok_ccsb, (CC, 1)], inc=CCD, k=16)
        P.add("G", lambda g: g.collective_compute(
            "AllReduce", ALU.max, replica_groups=[list(range(8))],
            ins=[cci.ap().opt()], outs=[cco.ap().opt()]),
            waits=[tok_cci], inc=CC)

        P.add("G", lambda g: g.dma_start(out=ONE[0:1, ccs_o:ccs_o + 16], in_=cco.ap()),
              waits=[(CC, 2)], inc=CCD, k=16)
        tok_ccback = (CCD, CCD.n)
        P.add("V", lambda v: v.tensor_single_scalar(out=ONE[0:1, am1_o:am1_o + 1],
                                                    in_=ONE[0:1, ccs_o:ccs_o + 1], scalar=1e-12,
                                                    op=ALU.max), waits=[tok_ccback], inc=VE, chain=True)
        P.add("V", lambda v: v.reciprocal(out=ONE[0:1, rec1_o:rec1_o + 1], in_=ONE[0:1, am1_o:am1_o + 1]), inc=VE, chain=True)
        P.add("V", lambda v: v.tensor_single_scalar(out=ONE[0:1, s1_o:s1_o + 1],
                                                    in_=ONE[0:1, rec1_o:rec1_o + 1], scalar=448.0,
                                                    op=ALU.mult), inc=VE, chain=True)
        P.add("V", lambda v: v.tensor_tensor(out=ONE[0:1, prod_o:prod_o + 1], in0=ONE[0:1, s1_o:s1_o + 1],
                                             in1=ONE[0:1, sw1_o:sw1_o + 1], op=ALU.mult), inc=VE, chain=True)
        P.add("V", lambda v: v.reciprocal(out=ONE[0:1, inv1_o:inv1_o + 1], in_=ONE[0:1, prod_o:prod_o + 1]), inc=VE, chain=True)
        tok_scalars = (VE, VE.n)
        P.add("T", lambda t: t.matmul(out=PS[:, 3072:3073], lhsT=ONE[0:1, ones_o:ones_o + 128],
                                      rhs=ONE[0:1, s1_o:s1_o + 1], start=True, stop=True),
              waits=[tok_scalars], inc=TE)
        P.add("T", lambda t: t.matmul(out=PS[:, 3074:3075], lhsT=ONE[0:1, ones_o:ones_o + 128],
                                      rhs=ONE[0:1, inv1_o:inv1_o + 1], start=True, stop=True), inc=TE)
        tok_bc = (TE, TE.n)
        P.add("V", lambda v: v.tensor_copy(out=AFR[:, scol_o:scol_o + 1], in_=PS[:, 3072:3073]),
              waits=[tok_bc], inc=VE)
        P.add("V", lambda v: v.tensor_copy(out=AFR[:, invcol_o:invcol_o + 1], in_=PS[:, 3074:3075]), inc=VE)
        P.add("V", lambda v: v.tensor_scalar(out=AFR[:, r3sa_o:r3sa_o + 16], in0=AFR[:, r3a_o:r3a_o + 16],
                                             scalar1=AFR[:, scol_o:scol_o + 1], scalar2=0.0,
                                             op0=ALU.mult, op1=ALU.add), inc=VE, chain=True)
        P.add("V", lambda v: v.tensor_scalar(out=AFR[:, nm3a_o:nm3a_o + 16], in0=AFR[:, nm3p_o:nm3p_o + 16],
                                             scalar1=AFR[:, scol_o:scol_o + 1], scalar2=0.0,
                                             op0=ALU.mult, op1=ALU.add), inc=VE, chain=True)
        tok_qscale = (VE, VE.n)

        ytp5 = [yt_o, xc_o]
        c1p5 = [c1_o, ln2t_o]
        etp5 = [et_o, x1ta_o]
        t4p5 = [t4t_o, x1tb_o]
        b8ap = [A8[:, b8_o:b8_o + 1024], AB[:, wslA:wslA + 512].bitcast(FP8)]
        a8ap = [A8[:, a8_o:a8_o + 1024], AB[:, wslA + 512:wslA + 1024].bitcast(FP8)]
        utp = [ut_o, pX1]; wt2p = [wt2_o, pX1 + 1024]
        t2tp = [t2t_o, pX2]; qtp = [qt_o, pX2 + 1024]
        xqfp = [xqf_o, pX3]
        pqvp = [PS[:, 2048:2560].bitcast(BF16), PS[:, 2560:3072].bitcast(BF16)]
        bnst4_o = [af(12), af(12)]; mv4s_o = [af(2), af(2)]
        u4_o = [af(1), af(1)]; rc4_o = [af(1), af(1)]; nm4_o = [af(1), af(1)]
        assert o2 <= 13200, o2

        tok_out = [NOTOK] * NTM
        tok_outv = [NOTOK] * NTM
        tok_etv = [NOTOK] * NTM
        tok_t4 = [NOTOK] * NTM
        tok_xqfS = [NOTOK] * NTM
        tok_ffn = [NOTOK] * NTM
        tok_trq = [NOTOK] * NTM
        tok_qtv = [NOTOK] * NTM
        tok_b8x = [NOTOK] * NTM
        tok_a8v = [NOTOK] * NTM
        tok_rstd4 = [NOTOK] * NTM
        tok_nmr4v = [NOTOK] * NTM

        def p5_q1(tc):
            p = tc % 2
            x1c = x1bf_o + tc * 1024
            tok_yt = P.add("G", lambda g, tc=tc, x1c=x1c, p=p: g.tensor_scalar(
                out=AFR[:, ytp5[p]:ytp5[p] + 1024], in0=AB[:, x1c:x1c + 1024],
                scalar1=AFR[:, r3sa_o + tc:r3sa_o + tc + 1],
                scalar2=AFR[:, nm3a_o + tc:nm3a_o + tc + 1], op0=ALU.mult, op1=ALU.add),
                waits=[tok_qscale if tc == 0 else NOTOK,
                       tok_b8x[tc - 2] if tc >= 2 else NOTOK], inc=GP)
            P.add("S", lambda s, p=p: s.activation(out=b8ap[p], in_=AFR[:, ytp5[p]:ytp5[p] + 1024],
                                                   func=AF.Identity, scale=AFR[:, cst_o + 5:cst_o + 6]),
                  waits=[tok_yt, tok_qtv[tc - 2] if tc >= 2 else NOTOK], inc=SE)
            tok_b8x[tc] = (SE, SE.n)
            tok_c1 = P.add("G", lambda g, p=p: g.tensor_scalar(
                out=AFR[:, c1p5[p]:c1p5[p] + 1024], in0=AFR[:, ytp5[p]:ytp5[p] + 1024],
                scalar1=128.0, scalar2=-128.0, op0=ALU.min, op1=ALU.max),
                waits=[tok_a8v[tc - 2] if tc >= 2 else NOTOK], inc=GP, chain=True)
            P.add("V", lambda v, p=p: v.tensor_copy(out=a8ap[p], in_=AFR[:, c1p5[p]:c1p5[p] + 1024]),
                  waits=[tok_c1], inc=VE)
            tok_a8v[tc] = (VE, VE.n)
            P.add("V", lambda v, p=p: v.tensor_scalar(out=AB[:, utp[p]:utp[p] + 1024], in0=b8ap[p],
                                                      scalar1=2.0, scalar2=128.0,
                                                      op0=ALU.mult, op1=ALU.min),
                  waits=[tok_b8x[tc]], inc=VE)
            P.add("V", lambda v, p=p: v.tensor_single_scalar(out=AB[:, wt2p[p]:wt2p[p] + 1024],
                                                             in_=AB[:, utp[p]:utp[p] + 1024],
                                                             scalar=-128.0, op=ALU.max), inc=VE)
            P.add("V", lambda v, p=p: v.scalar_tensor_tensor(out=AB[:, t2tp[p]:t2tp[p] + 1024],
                                                             in0=b8ap[p], scalar=2.0,
                                                             in1=AB[:, wt2p[p]:wt2p[p] + 1024],
                                                             op0=ALU.mult, op1=ALU.subtract), inc=VE)
            P.add("V", lambda v, p=p: v.scalar_tensor_tensor(out=AB[:, qtp[p]:qtp[p] + 1024],
                                                             in0=a8ap[p], scalar=1.0,
                                                             in1=AB[:, t2tp[p]:t2tp[p] + 1024],
                                                             op0=ALU.mult, op1=ALU.add),
                  waits=[tok_trq[tc - 2] if tc >= 2 else NOTOK], inc=VE)
            tok_qtv[tc] = (VE, VE.n)

        def p5_q2(tc):
            p = tc % 2
            for cb in range(8):
                wts = [tok_qtv[tc]] if cb == 0 else []
                if cb == 0 and tc >= 2:
                    wts.append(tok_xqfS[tc - 2])
                P.add("T", lambda t, cb=cb, p=p: t.transpose(
                    out=pqvp[p][:, cb * 128:(cb + 1) * 128],
                    in_=AB[:, qtp[p] + cb * 128: qtp[p] + (cb + 1) * 128],
                    identity=AB[:, idb_o:idb_o + 128]), waits=wts, inc=TE)
            tok_trq[tc] = (TE, TE.n)
            P.add("S", lambda s, p=p: s.activation(out=AB[:, xqfp[p]:xqfp[p] + 1024],
                                                   in_=pqvp[p][:, :], func=AF.Identity),
                  waits=[tok_trq[tc], tok_ffn[tc - 2] if tc >= 2 else NOTOK], inc=SE)
            tok_xqfS[tc] = (SE, SE.n)

        def p5_mm(tc):
            p = tc % 2
            pb = p * 1024
            for cb in range(8):
                wts = []
                if cb == 0:
                    wts = [tok_xqfS[tc]]
                    if tc >= 2:
                        wts.append(tok_etv[tc - 2])
                P.add("T", lambda t, cb=cb, pb=pb, p=p: t.matmul(
                    out=PS[:, pb:pb + 512], lhsT=AB[:, xqfp[p] + cb * 128: xqfp[p] + (cb + 1) * 128],
                    rhs=AB[:, qwb + cb * 1024: qwb + cb * 1024 + 512],
                    start=(cb == 0), stop=(cb == 7)), waits=wts, inc=TE)
                P.add("T", lambda t, cb=cb, pb=pb, p=p: t.matmul(
                    out=PS[:, pb + 512:pb + 1024], lhsT=AB[:, xqfp[p] + cb * 128: xqfp[p] + (cb + 1) * 128],
                    rhs=AB[:, qwb + cb * 1024 + 512: qwb + (cb + 1) * 1024],
                    start=(cb == 0), stop=(cb == 7)), inc=TE)
            tok_ffn[tc] = (TE, TE.n)

        def p5_ln(tc):
            p = tc % 2
            pb = p * 1024
            P.add("V", lambda v, pb=pb, p=p: v.scalar_tensor_tensor(
                out=AFR[:, etp5[p]:etp5[p] + 1024], in0=PS[:, pb:pb + 1024],
                scalar=AFR[:, invcol_o:invcol_o + 1], in1=AFR[:, biasb_o:biasb_o + 1024],
                op0=ALU.mult, op1=ALU.add),
                waits=[tok_ffn[tc], tok_t4[tc - 2] if tc >= 2 else NOTOK], inc=VE)
            tok_etv[tc] = (VE, VE.n)
            P.add("V", lambda v, p=p: v.bn_stats(out=AFR[:, bnst4_o[p]:bnst4_o[p] + 6],
                                                 in_=AFR[:, etp5[p]:etp5[p] + 512]), inc=VE)
            P.add("V", lambda v, p=p: v.bn_stats(out=AFR[:, bnst4_o[p] + 6:bnst4_o[p] + 12],
                                                 in_=AFR[:, etp5[p] + 512:etp5[p] + 1024]), inc=VE)
            P.add("V", lambda v, p=p: v.bn_aggr(out=AFR[:, mv4s_o[p]:mv4s_o[p] + 2],
                                                in_=AFR[:, bnst4_o[p]:bnst4_o[p] + 12]),
                  waits=[tok_rstd4[tc - 2]] if tc >= 2 else [], inc=VE, chain=True)
            tok_bn4v = (VE, VE.n)
            P.add("S", lambda s, p=p: s.activation(out=AFR[:, u4_o[p]:u4_o[p] + 1],
                                                   in_=AFR[:, mv4s_o[p] + 1:mv4s_o[p] + 2],
                                                   func=AF.Ln, bias=AFR[:, cst_o:cst_o + 1]),
                  waits=[tok_bn4v, tok_nmr4v[tc - 2] if tc >= 2 else NOTOK], inc=SE, chain=True)
            P.add("S", lambda s, p=p: s.activation(out=AFR[:, rc4_o[p]:rc4_o[p] + 1],
                                                   in_=AFR[:, u4_o[p]:u4_o[p] + 1], func=AF.Exp,
                                                   scale=AFR[:, cst_o + 4:cst_o + 5]), inc=SE, chain=True)
            tok_rstd4[tc] = (SE, SE.n)
            P.add("V", lambda v, p=p: v.tensor_scalar(out=AFR[:, nm4_o[p]:nm4_o[p] + 1],
                                                      in0=AFR[:, mv4s_o[p]:mv4s_o[p] + 1],
                                                      scalar1=AFR[:, rc4_o[p]:rc4_o[p] + 1], scalar2=-1.0,
                                                      op0=ALU.mult, op1=ALU.mult),
                  waits=[tok_rstd4[tc], tok_t4[tc - 2] if tc >= 2 else NOTOK], inc=VE, chain=True)
            tok_nmr4v[tc] = (VE, VE.n)

        def p5_tail_s(tc):
            p = tc % 2
            P.add("S", lambda s, p=p: s.activation(out=AFR[:, t4p5[p]:t4p5[p] + 1024],
                                                   in_=AFR[:, etp5[p]:etp5[p] + 1024],
                                                   func=AF.Identity, bias=AFR[:, nm4_o[p]:nm4_o[p] + 1],
                                                   scale=AFR[:, rc4_o[p]:rc4_o[p] + 1]),
                  waits=[tok_nmr4v[tc], tok_outv[tc - 2] if tc >= 2 else NOTOK], inc=SE, chain=True)
            tok_t4[tc] = (SE, SE.n)

        def p5_out(tc):
            p = tc % 2
            x1c = x1bf_o + tc * 1024
            oo = outa_o if p == 0 else outb_o
            go = GOA if p == 0 else GOB
            tok_outv[tc] = P.add("G", lambda g, x1c=x1c, oo=oo, p=p: g.tensor_tensor(
                out=AFR[:, oo:oo + 1024], in0=AB[:, x1c:x1c + 1024],
                in1=AFR[:, t4p5[p]:t4p5[p] + 1024], op=ALU.add),
                waits=[tok_t4[tc], tok_out[tc - 2] if tc >= 2 else NOTOK], inc=GP)
            tok_out[tc] = P.add("G", lambda g, tc=tc, oo=oo: g.dma_start(
                out=out_d[tc * 128:(tc + 1) * 128, :], in_=AFR[:, oo:oo + 1024]),
                waits=[tok_outv[tc]], inc=go, k=16)

        for r in range(NTM + 3):
            if 0 <= r - 3:
                p5_tail_s(r - 3)
            if r < NTM:
                p5_q1(r)
            if 0 <= r - 1 < NTM:
                p5_mm(r - 1)
            if 0 <= r - 2 < NTM:
                p5_ln(r - 2)
            if r < NTM:
                p5_q2(r)
            if 0 <= r - 3:
                p5_out(r - 3)
        goa_n, gob_n = GOA.n, GOB.n
        P.add("G", lambda g: (g.wait_ge(GOA.sem, goa_n), g.wait_ge(GOB.sem, gob_n)) and None)

        with nc.Block() as block:
            @block.sync
            def _(y):
                P.emit("Y", y)

            @block.gpsimd
            def _(g):
                P.emit("G", g)

            @block.vector
            def _(v):
                P.emit("V", v)

            @block.scalar
            def _(s):
                P.emit("S", s)

            @block.tensor
            def _(t):
                P.emit("T", t)

    return nc


def _prep_inputs(**inputs):
    x = np.asarray(inputs["L0_attn_x"], np.float32)
    td = np.asarray(inputs["L0_attn_time_decay"], np.float32)[:, 0]
    tf = np.asarray(inputs["L0_attn_time_first"], np.float32)[:, 0]
    tm = np.asarray(inputs["L0_attn_time_mix"], np.float32).reshape(1024)
    Wk = np.asarray(inputs["L0_attn_Wk"], np.float32)
    Wv = np.asarray(inputs["L0_attn_Wv"], np.float32)
    Wr = np.asarray(inputs["L0_attn_Wr"], np.float32)
    Wo = np.asarray(inputs["L0_attn_Wo"], np.float32)
    Wq = np.asarray(inputs["L0_ffn_weight"], np.float32)
    bias = np.asarray(inputs["L0_ffn_bias"], np.float32)

    def swz(Wm):
        return np.ascontiguousarray(
            Wm.T.reshape(8, 128, 1024).transpose(1, 0, 2).reshape(128, 8192))

    def col8(v):
        return np.ascontiguousarray(v.reshape(8, 128).T)

    shared = dict(
        wk=swz(Wk), wv=swz(Wv), wr=swz(Wr), wo=swz(Wo), wq=swz(Wq),
        tm=col8(tm), td=col8(td), tf=col8(tf),
        biasb=np.ascontiguousarray(np.broadcast_to(bias, (128, 1024))),
        idb=np.eye(128, dtype=ml_dtypes.bfloat16),
        idf=np.eye(128, dtype=np.float32),
        ones=np.ones((1, 128), np.float32),
    )
    in_maps = []
    for core in range(8):
        b, h = core // 2, core % 2
        t0 = h * T
        if h == 0:
            warm = np.zeros((W, 1024), np.float32)
        else:
            warm = x[b, t0 - W:t0]
        xc = np.ascontiguousarray(np.concatenate([warm, x[b, t0:t0 + T]], axis=0))
        m = dict(shared)
        m["x"] = xc
        m["mask"] = np.full((128, 1), float(h), np.float32)
        in_maps.append(m)
    return in_maps


def kernel(**inputs):
    in_maps = _prep_inputs(**inputs)
    nc = build_nc()
    res = run_bass_kernel_spmd(nc, in_maps, core_ids=list(range(8)))
    out = np.empty((4, 4096, 1024), np.float32)
    for core in range(8):
        b, h = core // 2, core % 2
        out[b, h * T:(h + 1) * T] = res.results[core]["out"]
    return out


# version 33
# speedup vs baseline: 1.0370x; 1.0370x over previous
import contextlib
import numpy as np
import ml_dtypes

import concourse.bass as bass
import concourse.mybir as mybir
from concourse.bass_utils import run_bass_kernel_spmd

F32 = mybir.dt.float32
BF16 = mybir.dt.bfloat16
FP8 = mybir.dt.float8e4
AF = mybir.ActivationFunctionType
ALU = mybir.AluOpType
AX = mybir.AxisListType

W = 128
T = 2048
TTOT = W + T
NT = TTOT // 128
NTM = T // 128
E60 = float(np.exp(np.float32(60.0)))
LN_EPS = 1e-5


class Ev:
    __slots__ = ("sem", "n")

    def __init__(self, sem):
        self.sem = sem
        self.n = 0


class Prog:
    def __init__(self):
        self.ops = {"Y": [], "G": [], "V": [], "S": [], "T": []}

    def add(self, eng, emit, waits=(), inc=None, k=1, chain=False):
        ws = list(waits)
        if chain and inc is not None:
            ws.append((inc, inc.n))
        tok = None
        if inc is not None:
            inc.n += k
            tok = (inc, inc.n)
        ws = tuple((ev, tv) for (ev, tv) in ws if ev is not None and tv > 0)
        self.ops[eng].append((ws, emit, inc, k))
        return tok

    def emit(self, name, eng):
        for waits, emit, inc, k in self.ops[name]:
            for ev, tokv in waits:
                eng.wait_ge(ev.sem, tokv)
            instr = emit(eng)
            if inc is not None and instr is not None:
                instr.then_inc(inc.sem, k)


NOTOK = (None, 0)


def build_nc():
    nc = bass.Bass("TRN2", target_bir_lowering=False, detect_race_conditions=False)

    def din(name, shape, dt=F32):
        return nc.dram_tensor(name, shape, dt, kind="ExternalInput").ap()

    x_d = din("x", [TTOT, 1024])
    wk_d = din("wk", [128, 8192])
    wv_d = din("wv", [128, 8192])
    wr_d = din("wr", [128, 8192])
    wo_d = din("wo", [128, 8192])
    wq_d = din("wq", [128, 8192])
    tm_d = din("tm", [128, 8])
    td_d = din("td", [128, 8])
    tf_d = din("tf", [128, 8])
    mask_d = din("mask", [128, 1])
    biasb_d = din("biasb", [128, 1024])
    idb_d = din("idb", [128, 128], BF16)
    idf_d = din("idf", [128, 128])
    ones_d = din("ones", [1, 128])

    out_d = nc.dram_tensor("out", [T, 1024], F32, kind="ExternalOutput").ap()
    cci = nc.dram_tensor("cci", [1, 16], F32)
    cco = nc.dram_tensor("cco", [1, 16], F32)

    es = contextlib.ExitStack()
    with es:
        AB = es.enter_context(nc.sbuf_tensor("AB", [128, 70700], BF16))
        AFR = es.enter_context(nc.sbuf_tensor("AF", [128, 13200], F32))
        A8 = es.enter_context(nc.sbuf_tensor("A8", [128, 2048], FP8))
        ONE = es.enter_context(nc.sbuf_tensor("ONE", [1, 192], F32))
        PS = es.enter_context(nc.psum_tensor("PS", [128, 4096], F32))

        o = 0
        def ab(n):
            nonlocal o
            r = o
            o += n
            return r
        wslA = ab(3072); wslB = ab(3072)
        wob = ab(8192); qwb = ab(8192)
        xmf = ab(8 * 2176)
        rwkb = ab(8 * 2048)
        pX1 = ab(2176); pX2 = ab(2176); pX3 = ab(2176)
        pY1 = ab(2048); pY2 = ab(2048); pY3 = ab(2048)
        vsb_o = ab(1024)
        idb_o = ab(128)
        vcar_o = ab(8)
        assert o <= 70700, o
        xlt0 = pX1; xlt1 = pX1 + 1024
        ut_o = pY1; wt2_o = pY1 + 1024
        t2t_o = pY2; qt_o = pY2 + 1024
        xqf_o = pY3
        x1bf_o = xmf

        o2 = 0
        def af(n):
            nonlocal o2
            r = o2
            o2 += n
            return r
        xa_o = af(1024); xb_o = af(1024); xc_o = af(1024)
        biasb_o = af(1024)
        bbuf_o = af(2052)
        bw_o = af(128)
        idf_o = af(128)
        ln2t_o = af(1024)
        yt_o = af(1024)
        c1_o = af(1024)
        et_o = af(1024)
        t4t_o = af(1024)
        bnst_o = af(12); ph1_o = af(2); ph2_o = af(2); mv2_o = af(2)
        dm_o = af(1); dm2_o = af(1); varf_o = af(1); u1_o = af(1)
        dmh_o = af(1); tmp16_o = af(16)
        meanf_o = mv2_o
        rcol_o = af(1); nmr_o = af(1)
        tmc_o = af(8); itm_o = af(8); tdc_o = af(8); tfc_o = af(8)
        dtmp_o = af(8); dcol_o = af(8); tfe_o = af(8)
        maskc_o = af(1)
        cst_o = af(8)
        m3a_o = af(16); r3a_o = af(16); amxa_o = af(16)
        r3sa_o = af(16); nm3a_o = af(16)
        wqam_o = af(8)
        amr_o = af(1); amw_o = af(1)
        scol_o = af(1); swcol_o = af(1); invcol_o = af(1)
        bnst2_o = [af(12), af(12)]; mv2s_o = [af(2), af(2)]
        u2_o = [af(1), af(1)]; rc2_o = [af(1), af(1)]; nm2_o = [af(1), af(1)]
        bnst3_o = [af(12), af(12)]; u3_o = [af(1), af(1)]
        mv3_o = af(32)
        nm3p_o = af(16)
        assert o2 <= 13200, o2
        x1ta_o = bbuf_o
        x1tb_o = bbuf_o + 1024
        outa_o = xa_o
        outb_o = xb_o

        ccsb_o = 0
        ccs_o = 16
        ract_o = 32; rw_o = 33
        am1_o = 34; rec1_o = 35; s1_o = 36
        aw1_o = 37; rec2_o = 38; sw1_o = 39
        prod_o = 40; inv1_o = 41
        amsc_o = 42
        ones_o = 64

        a8_o = 0; b8_o = 1024

        pcbv = [PS[:, cb * 512:(cb + 1) * 512].bitcast(BF16) for cb in range(8)]
        pqv = PS[:, 2048:2560].bitcast(BF16)

        sems = {}
        for s in ("LDA", "LDB", "LDC", "L1A", "L1B", "GDC", "G1A", "G1B",
                  "GOA", "GOB", "CCD", "TE", "VE", "SE", "CC", "GP"):
            sems[s] = Ev(es.enter_context(nc.semaphore(s)))
        LDA, LDB, LDC = sems["LDA"], sems["LDB"], sems["LDC"]
        L1A, L1B, GDC = sems["L1A"], sems["L1B"], sems["GDC"]
        G1A, G1B, GOA, GOB = sems["G1A"], sems["G1B"], sems["GOA"], sems["GOB"]
        CCD, TE, VE, SE, CC = sems["CCD"], sems["TE"], sems["VE"], sems["SE"], sems["CC"]
        GP = sems["GP"]

        P = Prog()

        def ln_combine():
            P.add("V", lambda v: v.bn_aggr(out=AFR[:, mv2_o:mv2_o + 2],
                                           in_=AFR[:, bnst_o:bnst_o + 12]), inc=VE, chain=True)

        def rstd_chain(extra_wait=NOTOK):
            w = [(VE, VE.n)]
            if extra_wait[0] is not None:
                w.append(extra_wait)
            P.add("S", lambda s: s.activation(out=AFR[:, u1_o:u1_o + 1], in_=AFR[:, mv2_o + 1:mv2_o + 2],
                                              func=AF.Ln, bias=AFR[:, cst_o:cst_o + 1]),
                  waits=w, inc=SE, chain=True)
            return P.add("S", lambda s: s.activation(out=AFR[:, rcol_o:rcol_o + 1],
                                                     in_=AFR[:, u1_o:u1_o + 1], func=AF.Exp,
                                                     scale=AFR[:, cst_o + 4:cst_o + 5]), inc=SE, chain=True)

        def nmr_op(tok_rstd):
            return P.add("V", lambda v: v.tensor_scalar(out=AFR[:, nmr_o:nmr_o + 1],
                                                        in0=AFR[:, meanf_o:meanf_o + 1],
                                                        scalar1=AFR[:, rcol_o:rcol_o + 1], scalar2=-1.0,
                                                        op0=ALU.mult, op1=ALU.mult),
                         waits=[tok_rstd], inc=VE, chain=True)

        for src_ap, dsto, ncols in ((tm_d, tmc_o, 8), (td_d, tdc_o, 8), (tf_d, tfc_o, 8),
                                    (mask_d, maskc_o, 1), (biasb_d, biasb_o, 1024)):
            P.add("G", lambda g, a=src_ap, b=dsto, n=ncols: g.dma_start(
                out=AFR[:, b:b + n], in_=a), inc=GDC, k=16)
        P.add("G", lambda g: g.dma_start(out=AB[:, idb_o:idb_o + 128], in_=idb_d), inc=GDC, k=16)
        P.add("G", lambda g: g.dma_start(out=ONE[0:1, ones_o:ones_o + 128], in_=ones_d), inc=GDC, k=16)
        tok_consts = (GDC, GDC.n)

        P.add("V", lambda v: v.tensor_scalar(out=AFR[:, itm_o:itm_o + 8], in0=AFR[:, tmc_o:tmc_o + 8],
                                             scalar1=-1.0, scalar2=1.0, op0=ALU.mult, op1=ALU.add),
              waits=[tok_consts], inc=VE)
        P.add("V", lambda v: v.memset(AB[:, vcar_o:vcar_o + 8], 0.0), inc=VE)
        P.add("V", lambda v: v.memset(ONE[0:1, ccsb_o:ccsb_o + 16], 0.0), inc=VE)
        for _i, _val in enumerate((LN_EPS, 1e-9, 1.0, -1.0, -0.5, 0.5)):
            P.add("V", lambda v, _i=_i, _val=_val: v.memset(AFR[:, cst_o + _i:cst_o + _i + 1], _val), inc=VE)
        tok_dconst_v = (VE, VE.n)
        P.add("S", lambda s: s.activation(out=AFR[:, dtmp_o:dtmp_o + 8], in_=AFR[:, tdc_o:tdc_o + 8],
                                          func=AF.Exp), waits=[tok_consts, tok_dconst_v], inc=SE)
        P.add("S", lambda s: s.activation(out=AFR[:, dcol_o:dcol_o + 8], in_=AFR[:, dtmp_o:dtmp_o + 8],
                                          func=AF.Exp, scale=AFR[:, cst_o + 3:cst_o + 4]), inc=SE, chain=True)
        P.add("S", lambda s: s.activation(out=AFR[:, tfe_o:tfe_o + 8], in_=AFR[:, tfc_o:tfc_o + 8],
                                          func=AF.Exp), inc=SE)

        P.add("G", lambda g: g.dma_start(out=cci[:, :], in_=ONE[0:1, ccsb_o:ccsb_o + 16]),
              waits=[tok_dconst_v], inc=CCD, k=16)
        P.add("G", lambda g: g.collective_compute(
            "AllReduce", ALU.max, replica_groups=[list(range(8))],
            ins=[cci.ap().opt()], outs=[cco.ap().opt()]),
            waits=[(CCD, 16)], inc=CC)

        wqslotF = [AB[:, qwb + ch * 2048: qwb + (ch + 1) * 2048].bitcast(F32) for ch in range(4)] \
            + [AB[:, wob + (ch - 4) * 2048: wob + (ch - 3) * 2048].bitcast(F32) for ch in range(4, 8)]
        tok_wql = [NOTOK] * 8
        tok_wqam = [NOTOK] * 8

        def wq_chunk(ch):
            tok_wql[ch] = P.add("Y", lambda y, ch=ch: y.dma_start(
                out=wqslotF[ch], in_=wq_d[:, ch * 1024:(ch + 1) * 1024]),
                waits=[tok_wqam[ch - 1] if ch else tok_dconst_v], inc=GDC, k=16)
            tok_wqam[ch] = P.add("V", lambda v, ch=ch: v.reduce_max(
                out=AFR[:, wqam_o + ch:wqam_o + ch + 1], in_=wqslotF[ch],
                axis=AX.X, apply_absolute_value=True),
                waits=[tok_wql[ch]], inc=VE)

        groups = [(0, 8), (8, 8), (16, 1)]
        tok_xload = [NOTOK] * NT
        tok_apply = [NOTOK] * NT
        tok_trans = [NOTOK] * NT
        tok_evac_g = [NOTOK] * len(groups)

        tok_rstd1 = [NOTOK] * NT
        tok_nmr1v = [NOTOK] * NT
        gi_of = {}
        for gi, (t0, glen) in enumerate(groups):
            for j in range(glen):
                gi_of[t0 + j] = (gi, j, t0, glen)

        def p1_a(i):
            p = i % 2
            xo = [xa_o, xb_o, xc_o][i % 3]
            ldev = [LDA, LDB, LDC][i % 3]
            wts = [tok_apply[i - 3]] if i >= 3 else []
            tok_xload[i] = P.add("Y", lambda y, i=i, xo=xo: y.dma_start(
                out=AFR[:, xo:xo + 1024], in_=x_d[i * 128:(i + 1) * 128, :]),
                waits=wts, inc=ldev, k=16)
            P.add("V", lambda v, xo=xo, p=p: v.bn_stats(out=AFR[:, bnst2_o[p]:bnst2_o[p] + 6],
                                                        in_=AFR[:, xo:xo + 512]),
                  waits=[tok_xload[i]], inc=VE)
            P.add("V", lambda v, xo=xo, p=p: v.bn_stats(out=AFR[:, bnst2_o[p] + 6:bnst2_o[p] + 12],
                                                        in_=AFR[:, xo + 512:xo + 1024]), inc=VE)
            P.add("V", lambda v, p=p: v.bn_aggr(out=AFR[:, mv2s_o[p]:mv2s_o[p] + 2],
                                                in_=AFR[:, bnst2_o[p]:bnst2_o[p] + 12]),
                  waits=[tok_rstd1[i - 2]] if i >= 2 else [], inc=VE, chain=True)
            tok_bn1v = (VE, VE.n)
            P.add("S", lambda s, p=p: s.activation(out=AFR[:, u2_o[p]:u2_o[p] + 1],
                                                   in_=AFR[:, mv2s_o[p] + 1:mv2s_o[p] + 2],
                                                   func=AF.Ln, bias=AFR[:, cst_o:cst_o + 1]),
                  waits=[tok_bn1v, tok_nmr1v[i - 2] if i >= 2 else NOTOK], inc=SE, chain=True)
            P.add("S", lambda s, p=p: s.activation(out=AFR[:, rc2_o[p]:rc2_o[p] + 1],
                                                   in_=AFR[:, u2_o[p]:u2_o[p] + 1], func=AF.Exp,
                                                   scale=AFR[:, cst_o + 4:cst_o + 5]), inc=SE, chain=True)
            tok_rstd1[i] = (SE, SE.n)

        def p1_b(i):
            p = i % 2
            xo = [xa_o, xb_o, xc_o][i % 3]
            xl = xlt0 if p == 0 else xlt1
            P.add("V", lambda v, p=p: v.tensor_scalar(out=AFR[:, nm2_o[p]:nm2_o[p] + 1],
                                                      in0=AFR[:, mv2s_o[p]:mv2s_o[p] + 1],
                                                      scalar1=AFR[:, rc2_o[p]:rc2_o[p] + 1], scalar2=-1.0,
                                                      op0=ALU.mult, op1=ALU.mult),
                  waits=[tok_rstd1[i], tok_apply[i - 2] if i >= 2 else NOTOK], inc=VE, chain=True)
            tok_nmr1v[i] = (VE, VE.n)
            wts = [tok_nmr1v[i]]
            if i >= 2:
                wts.append(tok_trans[i - 2])
            P.add("S", lambda s, xo=xo, xl=xl, p=p: s.activation(
                out=AB[:, xl:xl + 1024], in_=AFR[:, xo:xo + 1024], func=AF.Identity,
                bias=AFR[:, nm2_o[p]:nm2_o[p] + 1], scale=AFR[:, rc2_o[p]:rc2_o[p] + 1]),
                waits=wts, inc=SE, chain=True)
            tok_apply[i] = (SE, SE.n)

        def p1_c(i):
            gi, j, t0, glen = gi_of[i]
            p = i % 2
            xl = xlt0 if p == 0 else xlt1
            for cb in range(8):
                wts = [tok_apply[i]]
                if gi >= 1 and j == 0 and cb == 0:
                    wts.append(tok_evac_g[gi - 1])
                P.add("T", lambda t, xl=xl, cb=cb, j=j: t.transpose(
                    out=pcbv[cb][:, j * 128:(j + 1) * 128],
                    in_=AB[:, xl + cb * 128: xl + (cb + 1) * 128],
                    identity=AB[:, idb_o:idb_o + 128]),
                    waits=wts, inc=TE)
            tok_trans[i] = (TE, TE.n)
            if j != glen - 1:
                return
            glen128 = glen * 128
            for cb in range(8):
                xmo = xmf + cb * 2176 + t0 * 128
                wts = [tok_trans[t0 + glen - 1]]
                if cb == 0 and gi == 0:
                    wts.append(tok_dconst_v)
                P.add("V", lambda v, cb=cb, n=glen128: v.tensor_scalar(
                    out=AB[:, vsb_o:vsb_o + n], in0=pcbv[cb][:, 0:n],
                    scalar1=AFR[:, itm_o + cb:itm_o + cb + 1], scalar2=0.0,
                    op0=ALU.mult, op1=ALU.add),
                    waits=wts, inc=VE)
                P.add("V", lambda v, cb=cb, xmo=xmo, n=glen128: v.scalar_tensor_tensor(
                    out=AB[:, xmo + 1:xmo + n], in0=pcbv[cb][:, 1:n],
                    scalar=AFR[:, tmc_o + cb:tmc_o + cb + 1], in1=AB[:, vsb_o:vsb_o + n - 1],
                    op0=ALU.mult, op1=ALU.add), inc=VE)
                P.add("V", lambda v, cb=cb, xmo=xmo: v.scalar_tensor_tensor(
                    out=AB[:, xmo:xmo + 1], in0=pcbv[cb][:, 0:1],
                    scalar=AFR[:, tmc_o + cb:tmc_o + cb + 1], in1=AB[:, vcar_o + cb:vcar_o + cb + 1],
                    op0=ALU.mult, op1=ALU.add), inc=VE)
                P.add("V", lambda v, cb=cb, n=glen128: v.tensor_scalar(
                    out=AB[:, vcar_o + cb:vcar_o + cb + 1], in0=pcbv[cb][:, n - 1:n],
                    scalar1=AFR[:, itm_o + cb:itm_o + cb + 1], scalar2=0.0,
                    op0=ALU.mult, op1=ALU.add), inc=VE)
            tok_evac_g[gi] = (VE, VE.n)

        for r in range(NT + 1):
            if r < NT:
                p1_a(r)
            if 2 <= r < 10:
                wq_chunk(r - 2)
            if r >= 1:
                p1_b(r - 1)
                p1_c(r - 1)
        tok_xm_done = (VE, VE.n)

        P.add("V", lambda v: v.reduce_max(out=AFR[:, amw_o:amw_o + 1],
                                          in_=AFR[:, wqam_o:wqam_o + 8], axis=AX.X),
              waits=[tok_wqam[7]], inc=VE, chain=True)
        tok_amw = (VE, VE.n)
        P.add("G", lambda g: g.reduce_max(out=ONE[0:1, rw_o:rw_o + 1],
                                          in_=AFR[:, amw_o:amw_o + 1], axis=AX.XYZWC),
              waits=[tok_amw], inc=GP)
        tok_rw = (GP, GP.n)
        P.add("V", lambda v: v.tensor_single_scalar(out=ONE[0:1, aw1_o:aw1_o + 1],
                                                    in_=ONE[0:1, rw_o:rw_o + 1], scalar=1e-12,
                                                    op=ALU.max), waits=[tok_rw], inc=VE, chain=True)
        P.add("V", lambda v: v.reciprocal(out=ONE[0:1, rec2_o:rec2_o + 1], in_=ONE[0:1, aw1_o:aw1_o + 1]), inc=VE, chain=True)
        P.add("V", lambda v: v.tensor_single_scalar(out=ONE[0:1, sw1_o:sw1_o + 1],
                                                    in_=ONE[0:1, rec2_o:rec2_o + 1], scalar=448.0,
                                                    op=ALU.mult), inc=VE, chain=True)
        tok_sw1 = (VE, VE.n)
        P.add("T", lambda t: t.matmul(out=PS[:, 3073:3074], lhsT=ONE[0:1, ones_o:ones_o + 128],
                                      rhs=ONE[0:1, sw1_o:sw1_o + 1], start=True, stop=True),
              waits=[tok_sw1, tok_xm_done], inc=TE)
        tok_swbc = (TE, TE.n)
        P.add("V", lambda v: v.tensor_copy(out=AFR[:, swcol_o:swcol_o + 1], in_=PS[:, 3073:3074]),
              waits=[tok_swbc], inc=VE)
        tok_swcol = (VE, VE.n)

        tok_b8q = NOTOK
        tok_a8q = NOTOK
        tok_utq = NOTOK
        tok_qwq = NOTOK
        tok_qv_done = NOTOK

        def wq_quant_head(ch):
            nonlocal tok_b8q, tok_a8q
            prev_qwq = tok_qwq
            tok_c1q = P.add("G", lambda g, ch=ch: g.tensor_scalar(
                out=AFR[:, c1_o:c1_o + 1024], in0=wqslotF[ch],
                scalar1=AFR[:, swcol_o:swcol_o + 1], scalar2=0.0,
                op0=ALU.mult, op1=ALU.add),
                waits=[tok_wql[ch], tok_b8q, tok_swcol if ch == 0 else NOTOK], inc=GP)
            P.add("S", lambda s: s.activation(out=A8[:, b8_o:b8_o + 1024], in_=AFR[:, c1_o:c1_o + 1024],
                                              func=AF.Identity, scale=AFR[:, cst_o + 5:cst_o + 6]),
                  waits=[tok_c1q, prev_qwq], inc=SE)
            tok_b8q = (SE, SE.n)
            tok_ytq = P.add("G", lambda g: g.tensor_scalar(
                out=AFR[:, yt_o:yt_o + 1024], in0=AFR[:, c1_o:c1_o + 1024],
                scalar1=128.0, scalar2=-128.0, op0=ALU.min, op1=ALU.max),
                waits=[tok_a8q], inc=GP, chain=True)
            P.add("S", lambda s: s.activation(out=A8[:, a8_o:a8_o + 1024], in_=AFR[:, yt_o:yt_o + 1024],
                                              func=AF.Identity, scale=AFR[:, cst_o + 2:cst_o + 3]),
                  waits=[tok_ytq, prev_qwq], inc=SE)
            tok_a8q = (SE, SE.n)

        def wq_quant_tail(ch):
            nonlocal tok_qwq, tok_qv_done
            P.add("V", lambda v: v.tensor_scalar(
                out=AB[:, vsb_o:vsb_o + 1024], in0=A8[:, b8_o:b8_o + 1024],
                scalar1=2.0, scalar2=128.0, op0=ALU.mult, op1=ALU.min),
                waits=[tok_b8q], inc=VE)
            P.add("V", lambda v, ch=ch: v.tensor_single_scalar(out=AB[:, qwb + ch * 1024: qwb + (ch + 1) * 1024],
                                                               in_=AB[:, vsb_o:vsb_o + 1024], scalar=-128.0,
                                                               op=ALU.max), inc=VE, chain=True)
            P.add("V", lambda v, ch=ch: v.scalar_tensor_tensor(out=AB[:, vsb_o:vsb_o + 1024],
                                                               in0=A8[:, b8_o:b8_o + 1024], scalar=2.0,
                                                               in1=AB[:, qwb + ch * 1024: qwb + (ch + 1) * 1024],
                                                               op0=ALU.mult, op1=ALU.subtract), inc=VE, chain=True)
            P.add("V", lambda v, ch=ch: v.scalar_tensor_tensor(out=AB[:, qwb + ch * 1024: qwb + (ch + 1) * 1024],
                                                               in0=A8[:, a8_o:a8_o + 1024], scalar=1.0,
                                                               in1=AB[:, vsb_o:vsb_o + 1024], op0=ALU.mult,
                                                               op1=ALU.add),
                  waits=[tok_a8q], inc=VE, chain=True)
            tok_qwq = (VE, VE.n)
            tok_qv_done = tok_qwq

        wsl_bufs = [wslA, wslB]
        stage = [(xa_o, LDA), (xb_o, LDB), (xc_o, LDC)]
        tok_wdma = [[NOTOK] * 3 for _ in range(8)]
        tok_wcast = [NOTOK] * 8
        tok_wcast_wi = [[NOTOK] * 3 for _ in range(8)]
        tok_exp = [NOTOK] * 8
        tok_kv = [NOTOK] * 8
        tok_eneg = [NOTOK] * 8
        tok_kmm = [NOTOK] * 8
        tok_vmm = [NOTOK] * 8
        tok_rmm = [NOTOK] * 8

        def emit_wslice(db):
            wsl = wsl_bufs[db % 2]
            for wi, wd in enumerate((wk_d, wv_d, wr_d)):
                st, ldev = stage[wi]
                if db == 0:
                    wts = [tok_apply[NT - 1]]
                else:
                    wts = [tok_wcast_wi[db - 1][wi]]
                tok_wdma[db][wi] = P.add("Y", lambda y, wd=wd, st=st, db=db: y.dma_start(
                    out=AFR[:, st:st + 1024],
                    in_=wd.rearrange("p (cb d) -> p cb d", cb=8)[:, :, db * 128:(db + 1) * 128]),
                    waits=wts, inc=ldev, k=16)
                P.add("S", lambda s, st=st, wsl=wsl, wi=wi: s.activation(
                    out=AB[:, wsl + wi * 1024: wsl + (wi + 1) * 1024], in_=AFR[:, st:st + 1024],
                    func=AF.Identity),
                    waits=[tok_wdma[db][wi]], inc=SE)
                tok_wcast_wi[db][wi] = (SE, SE.n)
            tok_wcast[db] = (SE, SE.n)

        emit_wslice(0)

        tok_wo_cast = NOTOK

        def emit_wo():
            nonlocal tok_wo_cast
            tok_wo_cast_cb = [NOTOK] * 8
            for cb in range(8):
                st, ldev = (xa_o, LDA) if cb % 2 == 0 else (xb_o, LDB)
                wts = []
                if cb < 2:
                    wts.append(tok_wcast[7])
                if cb >= 2:
                    wts.append(tok_wo_cast_cb[cb - 2])
                tok_wo_dma = P.add("Y", lambda y, cb=cb, st=st: y.dma_start(
                    out=AFR[:, st:st + 1024],
                    in_=wo_d[:, cb * 1024:(cb + 1) * 1024]), waits=wts, inc=ldev, k=16)
                P.add("S", lambda s, cb=cb, st=st: s.activation(
                    out=AB[:, wob + cb * 1024: wob + (cb + 1) * 1024],
                    in_=AFR[:, st:st + 1024], func=AF.Identity),
                    waits=[tok_wo_dma, tok_qv_done if cb == 0 else NOTOK], inc=SE)
                tok_wo_cast_cb[cb] = (SE, SE.n)
            tok_wo_cast = (SE, SE.n)

        tok_sig_prev = NOTOK
        tok_rwkv_prev = NOTOK
        for db in range(8):
            wsl = wsl_bufs[db % 2]
            dcol_ap = AFR[:, dcol_o + db:dcol_o + db + 1]
            tfe_ap = AFR[:, tfe_o + db:tfe_o + db + 1]

            for cb in range(8):
                wts = []
                if cb == 0:
                    wts.append(tok_wcast[db])
                    wts.append(tok_xm_done)
                    if db > 0:
                        wts.append(tok_kv[db - 1])
                lw = wsl + 0 * 1024 + cb * 128
                xmb = xmf + cb * 2176
                for q in range(4):
                    P.add("T", lambda t, lw=lw, xmb=xmb, q=q, cb=cb: t.matmul(
                        out=PS[:, q * 512:(q + 1) * 512], lhsT=AB[:, lw:lw + 128],
                        rhs=AB[:, xmb + 128 + q * 512: xmb + 128 + (q + 1) * 512],
                        start=(cb == 0), stop=(cb == 7)),
                        waits=wts if q == 0 else (), inc=TE)
            tok_kmm_main = (TE, TE.n)
            for cb in range(8):
                wts = []
                if cb == 0 and db > 0:
                    wts.append(tok_eneg[db - 1])
                lw = wsl + 0 * 1024 + cb * 128
                xmb = xmf + cb * 2176
                P.add("T", lambda t, lw=lw, xmb=xmb, cb=cb: t.matmul(
                    out=PS[:, 2048:2176], lhsT=AB[:, lw:lw + 128], rhs=AB[:, xmb:xmb + 128],
                    start=(cb == 0), stop=(cb == 7)), waits=wts, inc=TE)
            tok_kmm[db] = (TE, TE.n)
            if db < 7:
                emit_wslice(db + 1)
            tok_exp_b = []
            for q in range(4):
                P.add("S", lambda s, q=q: s.activation(
                    out=AB[:, pX1 + 128 + q * 512:pX1 + 128 + (q + 1) * 512],
                    in_=PS[:, q * 512:(q + 1) * 512], func=AF.Exp),
                    waits=[tok_kmm_main] if q == 0 else [], inc=SE)
                tok_exp_b.append((SE, SE.n))
            P.add("S", lambda s: s.activation(out=AB[:, pX1:pX1 + 128], in_=PS[:, 2048:2176],
                                              func=AF.Exp), waits=[tok_kmm[db]], inc=SE)
            tok_exp[db] = (SE, SE.n)
            wq_quant_head(db)
            for cb in range(8):
                lw = wsl + 1 * 1024 + cb * 128
                xmb = xmf + cb * 2176
                for q in range(4):
                    P.add("T", lambda t, lw=lw, xmb=xmb, q=q, cb=cb: t.matmul(
                        out=PS[:, q * 512:(q + 1) * 512], lhsT=AB[:, lw:lw + 128],
                        rhs=AB[:, xmb + 128 + q * 512: xmb + 128 + (q + 1) * 512],
                        start=(cb == 0), stop=(cb == 7)),
                        waits=[tok_exp_b[q]] if cb == 0 else (), inc=TE)
            for cb in range(8):
                lw = wsl + 1 * 1024 + cb * 128
                xmb = xmf + cb * 2176
                P.add("T", lambda t, lw=lw, xmb=xmb, cb=cb: t.matmul(
                    out=PS[:, 2176:2304], lhsT=AB[:, lw:lw + 128], rhs=AB[:, xmb:xmb + 128],
                    start=(cb == 0), stop=(cb == 7)), inc=TE)
            tok_vmm[db] = (TE, TE.n)
            P.add("V", lambda v: v.scalar_tensor_tensor(out=AB[:, pX3:pX3 + 128],
                                                        in0=AB[:, pX1:pX1 + 128], scalar=E60,
                                                        in1=PS[:, 2176:2304],
                                                        op0=ALU.min, op1=ALU.mult),
                  waits=[tok_vmm[db], tok_exp[db], tok_rwkv_prev], inc=VE)
            P.add("V", lambda v: v.scalar_tensor_tensor(out=AB[:, pX3 + 128:pX3 + 2176],
                                                        in0=AB[:, pX1 + 128:pX1 + 2176], scalar=E60,
                                                        in1=PS[:, 0:2048],
                                                        op0=ALU.min, op1=ALU.mult), inc=VE)
            tok_kv[db] = (VE, VE.n)
            tok_min = P.add("V", lambda v: v.tensor_single_scalar(
                out=AB[:, pX2:pX2 + 2176], in_=AB[:, pX1:pX1 + 2176], scalar=E60,
                op=ALU.min),
                waits=[tok_sig_prev], inc=VE)
            for (src, dsty) in ((pX3, pY1), (pX2, pY2)):
                P.add("V", lambda v, d=dcol_ap, src=src: v.tensor_tensor_scan(
                    out=AFR[:, bw_o:bw_o + 128], data0=d.broadcast_to([128, 128]),
                    data1=AB[:, src:src + 128], initial=0.0, op0=ALU.mult, op1=ALU.add), inc=VE)
                P.add("V", lambda v: v.tensor_scalar(out=AFR[:, bbuf_o:bbuf_o + 1],
                                                     in0=AFR[:, bw_o + 127:bw_o + 128],
                                                     scalar1=AFR[:, maskc_o:maskc_o + 1], scalar2=0.0,
                                                     op0=ALU.mult, op1=ALU.add), inc=VE, chain=True)
                P.add("V", lambda v, d=dcol_ap, src=src: v.tensor_tensor_scan(
                    out=AFR[:, bbuf_o + 1:bbuf_o + 2049], data0=d.broadcast_to([128, 2048]),
                    data1=AB[:, src + 128:src + 2176], initial=AFR[:, bbuf_o:bbuf_o + 1],
                    op0=ALU.mult, op1=ALU.add), inc=VE, chain=True)
                P.add("V", lambda v, tf=tfe_ap, src=src, dsty=dsty: v.scalar_tensor_tensor(
                    out=AB[:, dsty:dsty + 2048], in0=AB[:, src + 128:src + 2176], scalar=tf,
                    in1=AFR[:, bbuf_o:bbuf_o + 2048], op0=ALU.mult, op1=ALU.add), inc=VE)
            tok_wq_v = (VE, VE.n)
            for cb in range(8):
                wts = []
                if cb == 0:
                    wts.append(tok_kv[db])
                    if db > 0:
                        wts.append(tok_eneg[db - 1])
                    else:
                        wts.append(tok_swcol)
                lw = wsl + 2 * 1024 + cb * 128
                xmb = xmf + cb * 2176
                for q in range(4):
                    P.add("T", lambda t, lw=lw, xmb=xmb, q=q, cb=cb: t.matmul(
                        out=PS[:, 2048 + q * 512:2048 + (q + 1) * 512], lhsT=AB[:, lw:lw + 128],
                        rhs=AB[:, xmb + 128 + q * 512: xmb + 128 + (q + 1) * 512],
                        start=(cb == 0), stop=(cb == 7)),
                        waits=wts if q == 0 else (), inc=TE)
            tok_rmm[db] = (TE, TE.n)
            P.add("S", lambda s: s.activation(out=AB[:, pX1:pX1 + 2048], in_=PS[:, 2048:4096],
                                              func=AF.Exp, scale=AFR[:, cst_o + 3:cst_o + 4]),
                  waits=[tok_rmm[db], tok_min], inc=SE)
            tok_eneg[db] = (SE, SE.n)
            P.add("S", lambda s: s.activation(out=AB[:, pY3:pY3 + 2048], in_=AB[:, pY2:pY2 + 2048],
                                              func=AF.Ln, bias=AFR[:, cst_o + 1:cst_o + 2]),
                  waits=[tok_wq_v, tok_rwkv_prev], inc=SE)
            P.add("S", lambda s: s.activation(out=AB[:, pY2:pY2 + 2048], in_=AB[:, pY3:pY3 + 2048],
                                              func=AF.Exp, scale=AFR[:, cst_o + 3:cst_o + 4]), inc=SE)
            tok_rcp = (SE, SE.n)
            P.add("G", lambda g: g.tensor_tensor(out=AB[:, pY3:pY3 + 2048], in0=AB[:, pY1:pY1 + 2048],
                                                 in1=AB[:, pY2:pY2 + 2048], op=ALU.mult),
                  waits=[tok_rcp], inc=GP)
            tok_ratio = (GP, GP.n)
            P.add("S", lambda s: s.activation(out=AB[:, pX2:pX2 + 2048], in_=AB[:, pX1:pX1 + 2048],
                                              func=AF.Ln, bias=AFR[:, cst_o + 2:cst_o + 3]),
                  waits=[tok_wq_v], inc=SE)
            P.add("S", lambda s: s.activation(out=AB[:, pX3:pX3 + 2048], in_=AB[:, pX2:pX2 + 2048],
                                              func=AF.Exp, scale=AFR[:, cst_o + 3:cst_o + 4]), inc=SE)
            tok_sig = (SE, SE.n)
            P.add("G", lambda g, db=db: g.tensor_tensor(
                out=AB[:, rwkb + db * 2048: rwkb + (db + 1) * 2048],
                in0=AB[:, pX3:pX3 + 2048], in1=AB[:, pY3:pY3 + 2048], op=ALU.mult),
                waits=[tok_sig], inc=GP)
            tok_rwkv_prev = (GP, GP.n)
            tok_sig_prev = tok_sig
            wq_quant_tail(db)
            if db == 7:
                emit_wo()
        tok_p2_v = tok_rwkv_prev
        tok_p2_s = (SE, SE.n)

        ln2tp = [ln2t_o, c1_o]
        yt3p = [yt_o, et_o]
        tok_xrel = [NOTOK] * NTM
        tok_womm = [NOTOK] * NTM
        tok_bn2v = [NOTOK] * NTM
        tok_rstd2 = [NOTOK] * NTM
        tok_nmr2v = [NOTOK] * NTM
        tok_ln2t = [NOTOK] * NTM
        tok_x1 = [NOTOK] * NTM
        tok_x1bf = [NOTOK] * NTM
        tok_rstd3 = [NOTOK] * NTM
        tok_p3v = [NOTOK] * NTM
        tok_amax = [NOTOK] * NTM

        def p3_stage_a(tc):
            p = tc % 2
            pb = p * 1024
            xro, ldev = (xa_o, LDA) if p == 0 else (xb_o, LDB)
            wts = [tok_wo_cast] if tc < 2 else [tok_x1[tc - 2]]
            tok_xrel[tc] = P.add("Y", lambda y, tc=tc, xro=xro: y.dma_start(
                out=AFR[:, xro:xro + 1024], in_=x_d[W + tc * 128: W + (tc + 1) * 128, :]),
                waits=wts, inc=ldev, k=16)
            for cb in range(8):
                wts = []
                if cb == 0:
                    wts.append(tok_wo_cast)
                    if tc == 0:
                        wts.append(tok_p2_v)
                    if tc >= 2:
                        wts.append(tok_ln2t[tc - 2])
                        wts.append(tok_bn2v[tc - 2])
                ro = rwkb + cb * 2048 + tc * 128
                P.add("T", lambda t, ro=ro, cb=cb, pb=pb: t.matmul(
                    out=PS[:, pb:pb + 512], lhsT=AB[:, ro:ro + 128],
                    rhs=AB[:, wob + cb * 1024: wob + cb * 1024 + 512],
                    start=(cb == 0), stop=(cb == 7)), waits=wts, inc=TE)
                P.add("T", lambda t, ro=ro, cb=cb, pb=pb: t.matmul(
                    out=PS[:, pb + 512:pb + 1024], lhsT=AB[:, ro:ro + 128],
                    rhs=AB[:, wob + cb * 1024 + 512: wob + (cb + 1) * 1024],
                    start=(cb == 0), stop=(cb == 7)), inc=TE)
            tok_womm[tc] = (TE, TE.n)
            P.add("V", lambda v, pb=pb, p=p: v.bn_stats(out=AFR[:, bnst2_o[p]:bnst2_o[p] + 6],
                                                        in_=PS[:, pb:pb + 512]),
                  waits=[tok_womm[tc]], inc=VE)
            P.add("V", lambda v, pb=pb, p=p: v.bn_stats(out=AFR[:, bnst2_o[p] + 6:bnst2_o[p] + 12],
                                                        in_=PS[:, pb + 512:pb + 1024]), inc=VE)
            P.add("V", lambda v, p=p: v.bn_aggr(out=AFR[:, mv2s_o[p]:mv2s_o[p] + 2],
                                                in_=AFR[:, bnst2_o[p]:bnst2_o[p] + 12]),
                  waits=[tok_rstd2[tc - 2]] if tc >= 2 else [], inc=VE, chain=True)
            tok_bn2v[tc] = (VE, VE.n)
            P.add("S", lambda s, p=p: s.activation(out=AFR[:, u2_o[p]:u2_o[p] + 1],
                                                   in_=AFR[:, mv2s_o[p] + 1:mv2s_o[p] + 2],
                                                   func=AF.Ln, bias=AFR[:, cst_o:cst_o + 1]),
                  waits=[tok_bn2v[tc], tok_nmr2v[tc - 2] if tc >= 2 else NOTOK], inc=SE, chain=True)
            P.add("S", lambda s, p=p: s.activation(out=AFR[:, rc2_o[p]:rc2_o[p] + 1],
                                                   in_=AFR[:, u2_o[p]:u2_o[p] + 1], func=AF.Exp,
                                                   scale=AFR[:, cst_o + 4:cst_o + 5]), inc=SE, chain=True)
            tok_rstd2[tc] = (SE, SE.n)

        def p3_stage_b(tc):
            p = tc % 2
            pb = p * 1024
            xro = xa_o if p == 0 else xb_o
            x1o = x1ta_o if p == 0 else x1tb_o
            P.add("V", lambda v, p=p: v.tensor_scalar(out=AFR[:, nm2_o[p]:nm2_o[p] + 1],
                                                      in0=AFR[:, mv2s_o[p]:mv2s_o[p] + 1],
                                                      scalar1=AFR[:, rc2_o[p]:rc2_o[p] + 1], scalar2=-1.0,
                                                      op0=ALU.mult, op1=ALU.mult),
                  waits=[tok_rstd2[tc], tok_ln2t[tc - 2] if tc >= 2 else NOTOK], inc=VE, chain=True)
            tok_nmr2v[tc] = (VE, VE.n)
            P.add("S", lambda s, pb=pb, p=p: s.activation(out=AFR[:, ln2tp[p]:ln2tp[p] + 1024],
                                                          in_=PS[:, pb:pb + 1024],
                                                          func=AF.Identity,
                                                          bias=AFR[:, nm2_o[p]:nm2_o[p] + 1],
                                                          scale=AFR[:, rc2_o[p]:rc2_o[p] + 1]),
                  waits=[tok_nmr2v[tc], tok_x1[tc - 2] if tc >= 2 else NOTOK], inc=SE, chain=True)
            tok_ln2t[tc] = (SE, SE.n)
            tok_x1[tc] = P.add("G", lambda g, xro=xro, x1o=x1o, p=p: g.tensor_tensor(
                out=AFR[:, x1o:x1o + 1024], in0=AFR[:, xro:xro + 1024],
                in1=AFR[:, ln2tp[p]:ln2tp[p] + 1024], op=ALU.add),
                waits=[tok_ln2t[tc], tok_xrel[tc],
                       tok_p3v[tc - 2] if tc >= 2 else NOTOK,
                       tok_x1bf[tc - 2] if tc >= 2 else NOTOK], inc=GP)

        def p3_stage_c(tc):
            p = tc % 2
            x1o = x1ta_o if p == 0 else x1tb_o
            P.add("S", lambda s, tc=tc, x1o=x1o: s.activation(
                out=AB[:, x1bf_o + tc * 1024: x1bf_o + (tc + 1) * 1024],
                in_=AFR[:, x1o:x1o + 1024], func=AF.Identity),
                waits=[tok_x1[tc]], inc=SE)
            tok_x1bf[tc] = (SE, SE.n)
            P.add("V", lambda v, x1o=x1o, p=p: v.bn_stats(out=AFR[:, bnst3_o[p]:bnst3_o[p] + 6],
                                                          in_=AFR[:, x1o:x1o + 512]),
                  waits=[tok_x1[tc]], inc=VE)
            P.add("V", lambda v, x1o=x1o, p=p: v.bn_stats(out=AFR[:, bnst3_o[p] + 6:bnst3_o[p] + 12],
                                                          in_=AFR[:, x1o + 512:x1o + 1024]), inc=VE)
            P.add("V", lambda v, tc=tc, p=p: v.bn_aggr(out=AFR[:, mv3_o + 2 * tc:mv3_o + 2 * tc + 2],
                                                       in_=AFR[:, bnst3_o[p]:bnst3_o[p] + 12]),
                  inc=VE, chain=True)
            tok_bn3v = (VE, VE.n)
            P.add("S", lambda s, tc=tc, p=p: s.activation(out=AFR[:, u3_o[p]:u3_o[p] + 1],
                                                          in_=AFR[:, mv3_o + 2 * tc + 1:mv3_o + 2 * tc + 2],
                                                          func=AF.Ln, bias=AFR[:, cst_o:cst_o + 1]),
                  waits=[tok_bn3v], inc=SE, chain=True)
            P.add("S", lambda s, tc=tc, p=p: s.activation(out=AFR[:, r3a_o + tc:r3a_o + tc + 1],
                                                          in_=AFR[:, u3_o[p]:u3_o[p] + 1], func=AF.Exp,
                                                          scale=AFR[:, cst_o + 4:cst_o + 5]),
                  inc=SE, chain=True)
            tok_rstd3[tc] = (SE, SE.n)

        def p3_stage_d(tc):
            p = tc % 2
            x1o = x1ta_o if p == 0 else x1tb_o
            P.add("V", lambda v, tc=tc: v.tensor_scalar(out=AFR[:, nm3p_o + tc:nm3p_o + tc + 1],
                                                        in0=AFR[:, mv3_o + 2 * tc:mv3_o + 2 * tc + 1],
                                                        scalar1=AFR[:, r3a_o + tc:r3a_o + tc + 1],
                                                        scalar2=-1.0,
                                                        op0=ALU.mult, op1=ALU.mult),
                  waits=[tok_rstd3[tc]], inc=VE, chain=True)
            P.add("V", lambda v, tc=tc, x1o=x1o, p=p: v.tensor_scalar(
                out=AFR[:, yt3p[p]:yt3p[p] + 1024], in0=AFR[:, x1o:x1o + 1024],
                scalar1=AFR[:, r3a_o + tc:r3a_o + tc + 1],
                scalar2=AFR[:, nm3p_o + tc:nm3p_o + tc + 1],
                op0=ALU.mult, op1=ALU.add),
                waits=[tok_amax[tc - 2] if tc >= 2 else NOTOK], inc=VE, chain=True)
            tok_p3v[tc] = (VE, VE.n)
            tok_amax[tc] = P.add("G", lambda g, tc=tc, p=p: g.reduce_max(
                out=ONE[0:1, amsc_o + tc:amsc_o + tc + 1], in_=AFR[:, yt3p[p]:yt3p[p] + 1024],
                axis=AX.XYZWC, apply_absolute_value=True),
                waits=[tok_p3v[tc]], inc=GP)

        for r in range(NTM + 1):
            if r < NTM:
                p3_stage_a(r)
            if r >= 1:
                p3_stage_c(r - 1)
            if r < NTM:
                p3_stage_b(r)
            if r >= 1:
                p3_stage_d(r - 1)

        P.add("G", lambda g: g.reduce_max(out=ONE[0:1, ccsb_o:ccsb_o + 1],
                                          in_=ONE[0:1, amsc_o:amsc_o + 16], axis=AX.XYZWC),
              inc=GP, chain=True)
        tok_ccsb = (GP, GP.n)
        tok_cci = P.add("G", lambda g: g.dma_start(out=cci[:, :], in_=ONE[0:1, ccsb_o:ccsb_o + 16]),
                        waits=[tok_ccsb, (CC, 1)], inc=CCD, k=16)
        P.add("G", lambda g: g.collective_compute(
            "AllReduce", ALU.max, replica_groups=[list(range(8))],
            ins=[cci.ap().opt()], outs=[cco.ap().opt()]),
            waits=[tok_cci], inc=CC)

        P.add("G", lambda g: g.dma_start(out=ONE[0:1, ccs_o:ccs_o + 16], in_=cco.ap()),
              waits=[(CC, 2)], inc=CCD, k=16)
        tok_ccback = (CCD, CCD.n)
        P.add("V", lambda v: v.tensor_single_scalar(out=ONE[0:1, am1_o:am1_o + 1],
                                                    in_=ONE[0:1, ccs_o:ccs_o + 1], scalar=1e-12,
                                                    op=ALU.max), waits=[tok_ccback], inc=VE, chain=True)
        P.add("V", lambda v: v.reciprocal(out=ONE[0:1, rec1_o:rec1_o + 1], in_=ONE[0:1, am1_o:am1_o + 1]), inc=VE, chain=True)
        P.add("V", lambda v: v.tensor_single_scalar(out=ONE[0:1, s1_o:s1_o + 1],
                                                    in_=ONE[0:1, rec1_o:rec1_o + 1], scalar=448.0,
                                                    op=ALU.mult), inc=VE, chain=True)
        P.add("V", lambda v: v.tensor_tensor(out=ONE[0:1, prod_o:prod_o + 1], in0=ONE[0:1, s1_o:s1_o + 1],
                                             in1=ONE[0:1, sw1_o:sw1_o + 1], op=ALU.mult), inc=VE, chain=True)
        P.add("V", lambda v: v.reciprocal(out=ONE[0:1, inv1_o:inv1_o + 1], in_=ONE[0:1, prod_o:prod_o + 1]), inc=VE, chain=True)
        tok_scalars = (VE, VE.n)
        P.add("T", lambda t: t.matmul(out=PS[:, 3072:3073], lhsT=ONE[0:1, ones_o:ones_o + 128],
                                      rhs=ONE[0:1, s1_o:s1_o + 1], start=True, stop=True),
              waits=[tok_scalars], inc=TE)
        P.add("T", lambda t: t.matmul(out=PS[:, 3074:3075], lhsT=ONE[0:1, ones_o:ones_o + 128],
                                      rhs=ONE[0:1, inv1_o:inv1_o + 1], start=True, stop=True), inc=TE)
        tok_bc = (TE, TE.n)
        P.add("V", lambda v: v.tensor_copy(out=AFR[:, scol_o:scol_o + 1], in_=PS[:, 3072:3073]),
              waits=[tok_bc], inc=VE)
        P.add("V", lambda v: v.tensor_copy(out=AFR[:, invcol_o:invcol_o + 1], in_=PS[:, 3074:3075]), inc=VE)
        P.add("V", lambda v: v.tensor_scalar(out=AFR[:, r3sa_o:r3sa_o + 16], in0=AFR[:, r3a_o:r3a_o + 16],
                                             scalar1=AFR[:, scol_o:scol_o + 1], scalar2=0.0,
                                             op0=ALU.mult, op1=ALU.add), inc=VE, chain=True)
        P.add("V", lambda v: v.tensor_scalar(out=AFR[:, nm3a_o:nm3a_o + 16], in0=AFR[:, nm3p_o:nm3p_o + 16],
                                             scalar1=AFR[:, scol_o:scol_o + 1], scalar2=0.0,
                                             op0=ALU.mult, op1=ALU.add), inc=VE, chain=True)
        tok_qscale = (VE, VE.n)

        ytp5 = [yt_o, xc_o]
        c1p5 = [c1_o, ln2t_o]
        etp5 = [et_o, x1ta_o]
        t4p5 = [t4t_o, x1tb_o]
        b8ap = [A8[:, b8_o:b8_o + 1024], AB[:, wslA:wslA + 512].bitcast(FP8)]
        a8ap = [A8[:, a8_o:a8_o + 1024], AB[:, wslA + 512:wslA + 1024].bitcast(FP8)]
        utp = [ut_o, pX1]; wt2p = [wt2_o, pX1 + 1024]
        t2tp = [t2t_o, pX2]; qtp = [qt_o, pX2 + 1024]
        xqfp = [xqf_o, pX3]
        pqvp = [PS[:, 2048:2560].bitcast(BF16), PS[:, 2560:3072].bitcast(BF16)]
        bnst4_o = [af(12), af(12)]; mv4s_o = [af(2), af(2)]
        u4_o = [af(1), af(1)]; rc4_o = [af(1), af(1)]; nm4_o = [af(1), af(1)]
        assert o2 <= 13200, o2

        tok_out = [NOTOK] * NTM
        tok_outv = [NOTOK] * NTM
        tok_etv = [NOTOK] * NTM
        tok_t4 = [NOTOK] * NTM
        tok_xqfS = [NOTOK] * NTM
        tok_ffn = [NOTOK] * NTM
        tok_trq = [NOTOK] * NTM
        tok_qtv = [NOTOK] * NTM
        tok_b8x = [NOTOK] * NTM
        tok_a8v = [NOTOK] * NTM
        tok_rstd4 = [NOTOK] * NTM
        tok_nmr4v = [NOTOK] * NTM

        def p5_q1(tc):
            p = tc % 2
            x1c = x1bf_o + tc * 1024
            tok_yt = P.add("G", lambda g, tc=tc, x1c=x1c, p=p: g.tensor_scalar(
                out=AFR[:, ytp5[p]:ytp5[p] + 1024], in0=AB[:, x1c:x1c + 1024],
                scalar1=AFR[:, r3sa_o + tc:r3sa_o + tc + 1],
                scalar2=AFR[:, nm3a_o + tc:nm3a_o + tc + 1], op0=ALU.mult, op1=ALU.add),
                waits=[tok_qscale if tc == 0 else NOTOK,
                       tok_b8x[tc - 2] if tc >= 2 else NOTOK], inc=GP)
            P.add("S", lambda s, p=p: s.activation(out=b8ap[p], in_=AFR[:, ytp5[p]:ytp5[p] + 1024],
                                                   func=AF.Identity, scale=AFR[:, cst_o + 5:cst_o + 6]),
                  waits=[tok_yt, tok_qtv[tc - 2] if tc >= 2 else NOTOK], inc=SE)
            tok_b8x[tc] = (SE, SE.n)
            tok_c1 = P.add("G", lambda g, p=p: g.tensor_scalar(
                out=AFR[:, c1p5[p]:c1p5[p] + 1024], in0=AFR[:, ytp5[p]:ytp5[p] + 1024],
                scalar1=128.0, scalar2=-128.0, op0=ALU.min, op1=ALU.max),
                waits=[tok_a8v[tc - 2] if tc >= 2 else NOTOK], inc=GP, chain=True)
            P.add("V", lambda v, p=p: v.tensor_copy(out=a8ap[p], in_=AFR[:, c1p5[p]:c1p5[p] + 1024]),
                  waits=[tok_c1], inc=VE)
            tok_a8v[tc] = (VE, VE.n)
            P.add("V", lambda v, p=p: v.tensor_scalar(out=AB[:, utp[p]:utp[p] + 1024], in0=b8ap[p],
                                                      scalar1=2.0, scalar2=128.0,
                                                      op0=ALU.mult, op1=ALU.min),
                  waits=[tok_b8x[tc]], inc=VE)
            P.add("V", lambda v, p=p: v.tensor_single_scalar(out=AB[:, wt2p[p]:wt2p[p] + 1024],
                                                             in_=AB[:, utp[p]:utp[p] + 1024],
                                                             scalar=-128.0, op=ALU.max), inc=VE)
            P.add("V", lambda v, p=p: v.scalar_tensor_tensor(out=AB[:, t2tp[p]:t2tp[p] + 1024],
                                                             in0=b8ap[p], scalar=2.0,
                                                             in1=AB[:, wt2p[p]:wt2p[p] + 1024],
                                                             op0=ALU.mult, op1=ALU.subtract), inc=VE)
            P.add("V", lambda v, p=p: v.scalar_tensor_tensor(out=AB[:, qtp[p]:qtp[p] + 1024],
                                                             in0=a8ap[p], scalar=1.0,
                                                             in1=AB[:, t2tp[p]:t2tp[p] + 1024],
                                                             op0=ALU.mult, op1=ALU.add),
                  waits=[tok_trq[tc - 2] if tc >= 2 else NOTOK], inc=VE)
            tok_qtv[tc] = (VE, VE.n)

        def p5_q2(tc):
            p = tc % 2
            for cb in range(8):
                wts = [tok_qtv[tc]] if cb == 0 else []
                if cb == 0 and tc >= 2:
                    wts.append(tok_xqfS[tc - 2])
                P.add("T", lambda t, cb=cb, p=p: t.transpose(
                    out=pqvp[p][:, cb * 128:(cb + 1) * 128],
                    in_=AB[:, qtp[p] + cb * 128: qtp[p] + (cb + 1) * 128],
                    identity=AB[:, idb_o:idb_o + 128]), waits=wts, inc=TE)
            tok_trq[tc] = (TE, TE.n)
            P.add("S", lambda s, p=p: s.activation(out=AB[:, xqfp[p]:xqfp[p] + 1024],
                                                   in_=pqvp[p][:, :], func=AF.Identity),
                  waits=[tok_trq[tc], tok_ffn[tc - 2] if tc >= 2 else NOTOK], inc=SE)
            tok_xqfS[tc] = (SE, SE.n)

        def p5_mm(tc):
            p = tc % 2
            pb = p * 1024
            for cb in range(8):
                wts = []
                if cb == 0:
                    wts = [tok_xqfS[tc]]
                    if tc >= 2:
                        wts.append(tok_etv[tc - 2])
                P.add("T", lambda t, cb=cb, pb=pb, p=p: t.matmul(
                    out=PS[:, pb:pb + 512], lhsT=AB[:, xqfp[p] + cb * 128: xqfp[p] + (cb + 1) * 128],
                    rhs=AB[:, qwb + cb * 1024: qwb + cb * 1024 + 512],
                    start=(cb == 0), stop=(cb == 7)), waits=wts, inc=TE)
                P.add("T", lambda t, cb=cb, pb=pb, p=p: t.matmul(
                    out=PS[:, pb + 512:pb + 1024], lhsT=AB[:, xqfp[p] + cb * 128: xqfp[p] + (cb + 1) * 128],
                    rhs=AB[:, qwb + cb * 1024 + 512: qwb + (cb + 1) * 1024],
                    start=(cb == 0), stop=(cb == 7)), inc=TE)
            tok_ffn[tc] = (TE, TE.n)

        def p5_ln(tc):
            p = tc % 2
            pb = p * 1024
            P.add("V", lambda v, pb=pb, p=p: v.scalar_tensor_tensor(
                out=AFR[:, etp5[p]:etp5[p] + 1024], in0=PS[:, pb:pb + 1024],
                scalar=AFR[:, invcol_o:invcol_o + 1], in1=AFR[:, biasb_o:biasb_o + 1024],
                op0=ALU.mult, op1=ALU.add),
                waits=[tok_ffn[tc], tok_t4[tc - 2] if tc >= 2 else NOTOK], inc=VE)
            tok_etv[tc] = (VE, VE.n)
            P.add("V", lambda v, p=p: v.bn_stats(out=AFR[:, bnst4_o[p]:bnst4_o[p] + 6],
                                                 in_=AFR[:, etp5[p]:etp5[p] + 512]), inc=VE)
            P.add("V", lambda v, p=p: v.bn_stats(out=AFR[:, bnst4_o[p] + 6:bnst4_o[p] + 12],
                                                 in_=AFR[:, etp5[p] + 512:etp5[p] + 1024]), inc=VE)
            P.add("V", lambda v, p=p: v.bn_aggr(out=AFR[:, mv4s_o[p]:mv4s_o[p] + 2],
                                                in_=AFR[:, bnst4_o[p]:bnst4_o[p] + 12]),
                  waits=[tok_rstd4[tc - 2]] if tc >= 2 else [], inc=VE, chain=True)
            tok_bn4v = (VE, VE.n)
            P.add("S", lambda s, p=p: s.activation(out=AFR[:, u4_o[p]:u4_o[p] + 1],
                                                   in_=AFR[:, mv4s_o[p] + 1:mv4s_o[p] + 2],
                                                   func=AF.Ln, bias=AFR[:, cst_o:cst_o + 1]),
                  waits=[tok_bn4v, tok_nmr4v[tc - 2] if tc >= 2 else NOTOK], inc=SE, chain=True)
            P.add("S", lambda s, p=p: s.activation(out=AFR[:, rc4_o[p]:rc4_o[p] + 1],
                                                   in_=AFR[:, u4_o[p]:u4_o[p] + 1], func=AF.Exp,
                                                   scale=AFR[:, cst_o + 4:cst_o + 5]), inc=SE, chain=True)
            tok_rstd4[tc] = (SE, SE.n)
            P.add("V", lambda v, p=p: v.tensor_scalar(out=AFR[:, nm4_o[p]:nm4_o[p] + 1],
                                                      in0=AFR[:, mv4s_o[p]:mv4s_o[p] + 1],
                                                      scalar1=AFR[:, rc4_o[p]:rc4_o[p] + 1], scalar2=-1.0,
                                                      op0=ALU.mult, op1=ALU.mult),
                  waits=[tok_rstd4[tc], tok_t4[tc - 2] if tc >= 2 else NOTOK], inc=VE, chain=True)
            tok_nmr4v[tc] = (VE, VE.n)

        def p5_tail_s(tc):
            p = tc % 2
            P.add("S", lambda s, p=p: s.activation(out=AFR[:, t4p5[p]:t4p5[p] + 1024],
                                                   in_=AFR[:, etp5[p]:etp5[p] + 1024],
                                                   func=AF.Identity, bias=AFR[:, nm4_o[p]:nm4_o[p] + 1],
                                                   scale=AFR[:, rc4_o[p]:rc4_o[p] + 1]),
                  waits=[tok_nmr4v[tc], tok_outv[tc - 2] if tc >= 2 else NOTOK], inc=SE, chain=True)
            tok_t4[tc] = (SE, SE.n)

        def p5_out(tc):
            p = tc % 2
            x1c = x1bf_o + tc * 1024
            oo = outa_o if p == 0 else outb_o
            go = GOA if p == 0 else GOB
            tok_outv[tc] = P.add("G", lambda g, x1c=x1c, oo=oo, p=p: g.tensor_tensor(
                out=AFR[:, oo:oo + 1024], in0=AB[:, x1c:x1c + 1024],
                in1=AFR[:, t4p5[p]:t4p5[p] + 1024], op=ALU.add),
                waits=[tok_t4[tc], tok_out[tc - 2] if tc >= 2 else NOTOK], inc=GP)
            tok_out[tc] = P.add("G", lambda g, tc=tc, oo=oo: g.dma_start(
                out=out_d[tc * 128:(tc + 1) * 128, :], in_=AFR[:, oo:oo + 1024]),
                waits=[tok_outv[tc]], inc=go, k=16)

        for r in range(NTM + 3):
            if 0 <= r - 3:
                p5_tail_s(r - 3)
            if r < NTM:
                p5_q1(r)
            if 0 <= r - 1 < NTM:
                p5_mm(r - 1)
            if r < NTM:
                p5_q2(r)
            if 0 <= r - 2 < NTM:
                p5_ln(r - 2)
            if 0 <= r - 3:
                p5_out(r - 3)
        goa_n, gob_n = GOA.n, GOB.n
        P.add("G", lambda g: (g.wait_ge(GOA.sem, goa_n), g.wait_ge(GOB.sem, gob_n)) and None)

        with nc.Block() as block:
            @block.sync
            def _(y):
                P.emit("Y", y)

            @block.gpsimd
            def _(g):
                P.emit("G", g)

            @block.vector
            def _(v):
                P.emit("V", v)

            @block.scalar
            def _(s):
                P.emit("S", s)

            @block.tensor
            def _(t):
                P.emit("T", t)

    return nc


def _prep_inputs(**inputs):
    x = np.asarray(inputs["L0_attn_x"], np.float32)
    td = np.asarray(inputs["L0_attn_time_decay"], np.float32)[:, 0]
    tf = np.asarray(inputs["L0_attn_time_first"], np.float32)[:, 0]
    tm = np.asarray(inputs["L0_attn_time_mix"], np.float32).reshape(1024)
    Wk = np.asarray(inputs["L0_attn_Wk"], np.float32)
    Wv = np.asarray(inputs["L0_attn_Wv"], np.float32)
    Wr = np.asarray(inputs["L0_attn_Wr"], np.float32)
    Wo = np.asarray(inputs["L0_attn_Wo"], np.float32)
    Wq = np.asarray(inputs["L0_ffn_weight"], np.float32)
    bias = np.asarray(inputs["L0_ffn_bias"], np.float32)

    def swz(Wm):
        return np.ascontiguousarray(
            Wm.T.reshape(8, 128, 1024).transpose(1, 0, 2).reshape(128, 8192))

    def col8(v):
        return np.ascontiguousarray(v.reshape(8, 128).T)

    shared = dict(
        wk=swz(Wk), wv=swz(Wv), wr=swz(Wr), wo=swz(Wo), wq=swz(Wq),
        tm=col8(tm), td=col8(td), tf=col8(tf),
        biasb=np.ascontiguousarray(np.broadcast_to(bias, (128, 1024))),
        idb=np.eye(128, dtype=ml_dtypes.bfloat16),
        idf=np.eye(128, dtype=np.float32),
        ones=np.ones((1, 128), np.float32),
    )
    in_maps = []
    for core in range(8):
        b, h = core // 2, core % 2
        t0 = h * T
        if h == 0:
            warm = np.zeros((W, 1024), np.float32)
        else:
            warm = x[b, t0 - W:t0]
        xc = np.ascontiguousarray(np.concatenate([warm, x[b, t0:t0 + T]], axis=0))
        m = dict(shared)
        m["x"] = xc
        m["mask"] = np.full((128, 1), float(h), np.float32)
        in_maps.append(m)
    return in_maps


def kernel(**inputs):
    in_maps = _prep_inputs(**inputs)
    nc = build_nc()
    res = run_bass_kernel_spmd(nc, in_maps, core_ids=list(range(8)))
    out = np.empty((4, 4096, 1024), np.float32)
    for core in range(8):
        b, h = core // 2, core % 2
        out[b, h * T:(h + 1) * T] = res.results[core]["out"]
    return out


# version 34 (final)
# speedup vs baseline: 1.0499x; 1.0125x over previous
import contextlib
import numpy as np
import ml_dtypes

import concourse.bass as bass
import concourse.mybir as mybir
from concourse.bass_utils import run_bass_kernel_spmd

F32 = mybir.dt.float32
BF16 = mybir.dt.bfloat16
FP8 = mybir.dt.float8e4
AF = mybir.ActivationFunctionType
ALU = mybir.AluOpType
AX = mybir.AxisListType

W = 128
T = 2048
TTOT = W + T
NT = TTOT // 128
NTM = T // 128
E60 = float(np.exp(np.float32(60.0)))
LN_EPS = 1e-5


class Ev:
    __slots__ = ("sem", "n")

    def __init__(self, sem):
        self.sem = sem
        self.n = 0


class Prog:
    def __init__(self):
        self.ops = {"Y": [], "G": [], "V": [], "S": [], "T": []}

    def add(self, eng, emit, waits=(), inc=None, k=1, chain=False):
        ws = list(waits)
        if chain and inc is not None:
            ws.append((inc, inc.n))
        tok = None
        if inc is not None:
            inc.n += k
            tok = (inc, inc.n)
        ws = tuple((ev, tv) for (ev, tv) in ws if ev is not None and tv > 0)
        self.ops[eng].append((ws, emit, inc, k))
        return tok

    def emit(self, name, eng):
        for waits, emit, inc, k in self.ops[name]:
            for ev, tokv in waits:
                eng.wait_ge(ev.sem, tokv)
            instr = emit(eng)
            if inc is not None and instr is not None:
                instr.then_inc(inc.sem, k)


NOTOK = (None, 0)


def build_nc():
    nc = bass.Bass("TRN2", target_bir_lowering=False, detect_race_conditions=False)

    def din(name, shape, dt=F32):
        return nc.dram_tensor(name, shape, dt, kind="ExternalInput").ap()

    x_d = din("x", [TTOT, 1024])
    wk_d = din("wk", [128, 8192])
    wv_d = din("wv", [128, 8192])
    wr_d = din("wr", [128, 8192])
    wo_d = din("wo", [128, 8192])
    wq_d = din("wq", [128, 8192])
    tm_d = din("tm", [128, 8])
    td_d = din("td", [128, 8])
    tf_d = din("tf", [128, 8])
    mask_d = din("mask", [128, 1])
    biasb_d = din("biasb", [128, 1024])
    idb_d = din("idb", [128, 128], BF16)
    idf_d = din("idf", [128, 128])
    ones_d = din("ones", [1, 128])

    out_d = nc.dram_tensor("out", [T, 1024], F32, kind="ExternalOutput").ap()
    cci = nc.dram_tensor("cci", [1, 16], F32)
    cco = nc.dram_tensor("cco", [1, 16], F32)

    es = contextlib.ExitStack()
    with es:
        AB = es.enter_context(nc.sbuf_tensor("AB", [128, 70700], BF16))
        AFR = es.enter_context(nc.sbuf_tensor("AF", [128, 13200], F32))
        A8 = es.enter_context(nc.sbuf_tensor("A8", [128, 2048], FP8))
        ONE = es.enter_context(nc.sbuf_tensor("ONE", [1, 192], F32))
        PS = es.enter_context(nc.psum_tensor("PS", [128, 4096], F32))

        o = 0
        def ab(n):
            nonlocal o
            r = o
            o += n
            return r
        wslA = ab(3072); wslB = ab(3072)
        wob = ab(8192); qwb = ab(8192)
        xmf = ab(8 * 2176)
        rwkb = ab(8 * 2048)
        pX1 = ab(2176); pX2 = ab(2176); pX3 = ab(2176)
        pY1 = ab(2048); pY2 = ab(2048); pY3 = ab(2048)
        vsb_o = ab(1024)
        idb_o = ab(128)
        vcar_o = ab(8)
        assert o <= 70700, o
        xlt0 = pX1; xlt1 = pX1 + 1024
        ut_o = pY1; wt2_o = pY1 + 1024
        t2t_o = pY2; qt_o = pY2 + 1024
        xqf_o = pY3
        x1bf_o = xmf

        o2 = 0
        def af(n):
            nonlocal o2
            r = o2
            o2 += n
            return r
        xa_o = af(1024); xb_o = af(1024); xc_o = af(1024)
        biasb_o = af(1024)
        bbuf_o = af(2052)
        bw_o = af(128)
        idf_o = af(128)
        ln2t_o = af(1024)
        yt_o = af(1024)
        c1_o = af(1024)
        et_o = af(1024)
        t4t_o = af(1024)
        bnst_o = af(12); ph1_o = af(2); ph2_o = af(2); mv2_o = af(2)
        dm_o = af(1); dm2_o = af(1); varf_o = af(1); u1_o = af(1)
        dmh_o = af(1); tmp16_o = af(16)
        meanf_o = mv2_o
        rcol_o = af(1); nmr_o = af(1)
        tmc_o = af(8); itm_o = af(8); tdc_o = af(8); tfc_o = af(8)
        dtmp_o = af(8); dcol_o = af(8); tfe_o = af(8)
        maskc_o = af(1)
        cst_o = af(8)
        m3a_o = af(16); r3a_o = af(16); amxa_o = af(16)
        r3sa_o = af(16); nm3a_o = af(16)
        wqam_o = af(8)
        amr_o = af(1); amw_o = af(1)
        scol_o = af(1); swcol_o = af(1); invcol_o = af(1)
        bnst2_o = [af(12), af(12)]; mv2s_o = [af(2), af(2)]
        u2_o = [af(1), af(1)]; rc2_o = [af(1), af(1)]; nm2_o = [af(1), af(1)]
        bnst3_o = [af(12), af(12)]; u3_o = [af(1), af(1)]
        mv3_o = af(32)
        nm3p_o = af(16)
        assert o2 <= 13200, o2
        x1ta_o = bbuf_o
        x1tb_o = bbuf_o + 1024
        outa_o = xa_o
        outb_o = xb_o

        ccsb_o = 0
        ccs_o = 16
        ract_o = 32; rw_o = 33
        am1_o = 34; rec1_o = 35; s1_o = 36
        aw1_o = 37; rec2_o = 38; sw1_o = 39
        prod_o = 40; inv1_o = 41
        amsc_o = 42
        ones_o = 64

        a8_o = 0; b8_o = 1024

        pcbv = [PS[:, cb * 512:(cb + 1) * 512].bitcast(BF16) for cb in range(8)]
        pqv = PS[:, 2048:2560].bitcast(BF16)

        sems = {}
        for s in ("LDA", "LDB", "LDC", "L1A", "L1B", "GDC", "G1A", "G1B",
                  "GOA", "GOB", "CCD", "TE", "VE", "SE", "CC", "GP"):
            sems[s] = Ev(es.enter_context(nc.semaphore(s)))
        LDA, LDB, LDC = sems["LDA"], sems["LDB"], sems["LDC"]
        L1A, L1B, GDC = sems["L1A"], sems["L1B"], sems["GDC"]
        G1A, G1B, GOA, GOB = sems["G1A"], sems["G1B"], sems["GOA"], sems["GOB"]
        CCD, TE, VE, SE, CC = sems["CCD"], sems["TE"], sems["VE"], sems["SE"], sems["CC"]
        GP = sems["GP"]

        P = Prog()

        def ln_combine():
            P.add("V", lambda v: v.bn_aggr(out=AFR[:, mv2_o:mv2_o + 2],
                                           in_=AFR[:, bnst_o:bnst_o + 12]), inc=VE, chain=True)

        def rstd_chain(extra_wait=NOTOK):
            w = [(VE, VE.n)]
            if extra_wait[0] is not None:
                w.append(extra_wait)
            P.add("S", lambda s: s.activation(out=AFR[:, u1_o:u1_o + 1], in_=AFR[:, mv2_o + 1:mv2_o + 2],
                                              func=AF.Ln, bias=AFR[:, cst_o:cst_o + 1]),
                  waits=w, inc=SE, chain=True)
            return P.add("S", lambda s: s.activation(out=AFR[:, rcol_o:rcol_o + 1],
                                                     in_=AFR[:, u1_o:u1_o + 1], func=AF.Exp,
                                                     scale=AFR[:, cst_o + 4:cst_o + 5]), inc=SE, chain=True)

        def nmr_op(tok_rstd):
            return P.add("V", lambda v: v.tensor_scalar(out=AFR[:, nmr_o:nmr_o + 1],
                                                        in0=AFR[:, meanf_o:meanf_o + 1],
                                                        scalar1=AFR[:, rcol_o:rcol_o + 1], scalar2=-1.0,
                                                        op0=ALU.mult, op1=ALU.mult),
                         waits=[tok_rstd], inc=VE, chain=True)

        for src_ap, dsto, ncols in ((tm_d, tmc_o, 8), (td_d, tdc_o, 8), (tf_d, tfc_o, 8),
                                    (mask_d, maskc_o, 1), (biasb_d, biasb_o, 1024)):
            P.add("G", lambda g, a=src_ap, b=dsto, n=ncols: g.dma_start(
                out=AFR[:, b:b + n], in_=a), inc=GDC, k=16)
        P.add("G", lambda g: g.dma_start(out=AB[:, idb_o:idb_o + 128], in_=idb_d), inc=GDC, k=16)
        P.add("G", lambda g: g.dma_start(out=ONE[0:1, ones_o:ones_o + 128], in_=ones_d), inc=GDC, k=16)
        tok_consts = (GDC, GDC.n)

        P.add("V", lambda v: v.tensor_scalar(out=AFR[:, itm_o:itm_o + 8], in0=AFR[:, tmc_o:tmc_o + 8],
                                             scalar1=-1.0, scalar2=1.0, op0=ALU.mult, op1=ALU.add),
              waits=[tok_consts], inc=VE)
        P.add("V", lambda v: v.memset(AB[:, vcar_o:vcar_o + 8], 0.0), inc=VE)
        P.add("V", lambda v: v.memset(ONE[0:1, ccsb_o:ccsb_o + 16], 0.0), inc=VE)
        for _i, _val in enumerate((LN_EPS, 1e-9, 1.0, -1.0, -0.5, 0.5)):
            P.add("V", lambda v, _i=_i, _val=_val: v.memset(AFR[:, cst_o + _i:cst_o + _i + 1], _val), inc=VE)
        tok_dconst_v = (VE, VE.n)
        P.add("S", lambda s: s.activation(out=AFR[:, dtmp_o:dtmp_o + 8], in_=AFR[:, tdc_o:tdc_o + 8],
                                          func=AF.Exp), waits=[tok_consts, tok_dconst_v], inc=SE)
        P.add("S", lambda s: s.activation(out=AFR[:, dcol_o:dcol_o + 8], in_=AFR[:, dtmp_o:dtmp_o + 8],
                                          func=AF.Exp, scale=AFR[:, cst_o + 3:cst_o + 4]), inc=SE, chain=True)
        P.add("S", lambda s: s.activation(out=AFR[:, tfe_o:tfe_o + 8], in_=AFR[:, tfc_o:tfc_o + 8],
                                          func=AF.Exp), inc=SE)

        P.add("G", lambda g: g.dma_start(out=cci[:, :], in_=ONE[0:1, ccsb_o:ccsb_o + 16]),
              waits=[tok_dconst_v], inc=CCD, k=16)
        P.add("G", lambda g: g.collective_compute(
            "AllReduce", ALU.max, replica_groups=[list(range(8))],
            ins=[cci.ap().opt()], outs=[cco.ap().opt()]),
            waits=[(CCD, 16)], inc=CC)

        wqslotF = [AB[:, qwb + ch * 2048: qwb + (ch + 1) * 2048].bitcast(F32) for ch in range(4)] \
            + [AB[:, wob + (ch - 4) * 2048: wob + (ch - 3) * 2048].bitcast(F32) for ch in range(4, 8)]
        tok_wql = [NOTOK] * 8
        tok_wqam = [NOTOK] * 8

        def wq_chunk(ch):
            tok_wql[ch] = P.add("Y", lambda y, ch=ch: y.dma_start(
                out=wqslotF[ch], in_=wq_d[:, ch * 1024:(ch + 1) * 1024]),
                waits=[tok_wqam[ch - 1] if ch else tok_dconst_v], inc=GDC, k=16)
            tok_wqam[ch] = P.add("V", lambda v, ch=ch: v.reduce_max(
                out=AFR[:, wqam_o + ch:wqam_o + ch + 1], in_=wqslotF[ch],
                axis=AX.X, apply_absolute_value=True),
                waits=[tok_wql[ch]], inc=VE)

        groups = [(0, 8), (8, 8), (16, 1)]
        tok_xload = [NOTOK] * NT
        tok_apply = [NOTOK] * NT
        tok_trans = [NOTOK] * NT
        tok_evac_g = [NOTOK] * len(groups)

        tok_rstd1 = [NOTOK] * NT
        tok_nmr1v = [NOTOK] * NT
        gi_of = {}
        for gi, (t0, glen) in enumerate(groups):
            for j in range(glen):
                gi_of[t0 + j] = (gi, j, t0, glen)

        def p1_a(i):
            p = i % 2
            xo = [xa_o, xb_o, xc_o][i % 3]
            ldev = [LDA, LDB, LDC][i % 3]
            wts = [tok_apply[i - 3]] if i >= 3 else []
            tok_xload[i] = P.add("Y", lambda y, i=i, xo=xo: y.dma_start(
                out=AFR[:, xo:xo + 1024], in_=x_d[i * 128:(i + 1) * 128, :]),
                waits=wts, inc=ldev, k=16)
            P.add("V", lambda v, xo=xo, p=p: v.bn_stats(out=AFR[:, bnst2_o[p]:bnst2_o[p] + 6],
                                                        in_=AFR[:, xo:xo + 512]),
                  waits=[tok_xload[i]], inc=VE)
            P.add("V", lambda v, xo=xo, p=p: v.bn_stats(out=AFR[:, bnst2_o[p] + 6:bnst2_o[p] + 12],
                                                        in_=AFR[:, xo + 512:xo + 1024]), inc=VE)
            P.add("V", lambda v, p=p: v.bn_aggr(out=AFR[:, mv2s_o[p]:mv2s_o[p] + 2],
                                                in_=AFR[:, bnst2_o[p]:bnst2_o[p] + 12]),
                  waits=[tok_rstd1[i - 2]] if i >= 2 else [], inc=VE, chain=True)
            tok_bn1v = (VE, VE.n)
            P.add("S", lambda s, p=p: s.activation(out=AFR[:, u2_o[p]:u2_o[p] + 1],
                                                   in_=AFR[:, mv2s_o[p] + 1:mv2s_o[p] + 2],
                                                   func=AF.Ln, bias=AFR[:, cst_o:cst_o + 1]),
                  waits=[tok_bn1v, tok_nmr1v[i - 2] if i >= 2 else NOTOK], inc=SE, chain=True)
            P.add("S", lambda s, p=p: s.activation(out=AFR[:, rc2_o[p]:rc2_o[p] + 1],
                                                   in_=AFR[:, u2_o[p]:u2_o[p] + 1], func=AF.Exp,
                                                   scale=AFR[:, cst_o + 4:cst_o + 5]), inc=SE, chain=True)
            tok_rstd1[i] = (SE, SE.n)

        def p1_b(i):
            p = i % 2
            xo = [xa_o, xb_o, xc_o][i % 3]
            xl = xlt0 if p == 0 else xlt1
            P.add("V", lambda v, p=p: v.tensor_scalar(out=AFR[:, nm2_o[p]:nm2_o[p] + 1],
                                                      in0=AFR[:, mv2s_o[p]:mv2s_o[p] + 1],
                                                      scalar1=AFR[:, rc2_o[p]:rc2_o[p] + 1], scalar2=-1.0,
                                                      op0=ALU.mult, op1=ALU.mult),
                  waits=[tok_rstd1[i], tok_apply[i - 2] if i >= 2 else NOTOK], inc=VE, chain=True)
            tok_nmr1v[i] = (VE, VE.n)
            wts = [tok_nmr1v[i]]
            if i >= 2:
                wts.append(tok_trans[i - 2])
            P.add("S", lambda s, xo=xo, xl=xl, p=p: s.activation(
                out=AB[:, xl:xl + 1024], in_=AFR[:, xo:xo + 1024], func=AF.Identity,
                bias=AFR[:, nm2_o[p]:nm2_o[p] + 1], scale=AFR[:, rc2_o[p]:rc2_o[p] + 1]),
                waits=wts, inc=SE, chain=True)
            tok_apply[i] = (SE, SE.n)

        def p1_c(i):
            gi, j, t0, glen = gi_of[i]
            p = i % 2
            xl = xlt0 if p == 0 else xlt1
            for cb in range(8):
                wts = [tok_apply[i]]
                if gi >= 1 and j == 0 and cb == 0:
                    wts.append(tok_evac_g[gi - 1])
                P.add("T", lambda t, xl=xl, cb=cb, j=j: t.transpose(
                    out=pcbv[cb][:, j * 128:(j + 1) * 128],
                    in_=AB[:, xl + cb * 128: xl + (cb + 1) * 128],
                    identity=AB[:, idb_o:idb_o + 128]),
                    waits=wts, inc=TE)
            tok_trans[i] = (TE, TE.n)
            if j != glen - 1:
                return
            glen128 = glen * 128
            for cb in range(8):
                xmo = xmf + cb * 2176 + t0 * 128
                wts = [tok_trans[t0 + glen - 1]]
                if cb == 0 and gi == 0:
                    wts.append(tok_dconst_v)
                P.add("V", lambda v, cb=cb, n=glen128: v.tensor_scalar(
                    out=AB[:, vsb_o:vsb_o + n], in0=pcbv[cb][:, 0:n],
                    scalar1=AFR[:, itm_o + cb:itm_o + cb + 1], scalar2=0.0,
                    op0=ALU.mult, op1=ALU.add),
                    waits=wts, inc=VE)
                P.add("V", lambda v, cb=cb, xmo=xmo, n=glen128: v.scalar_tensor_tensor(
                    out=AB[:, xmo + 1:xmo + n], in0=pcbv[cb][:, 1:n],
                    scalar=AFR[:, tmc_o + cb:tmc_o + cb + 1], in1=AB[:, vsb_o:vsb_o + n - 1],
                    op0=ALU.mult, op1=ALU.add), inc=VE)
                P.add("V", lambda v, cb=cb, xmo=xmo: v.scalar_tensor_tensor(
                    out=AB[:, xmo:xmo + 1], in0=pcbv[cb][:, 0:1],
                    scalar=AFR[:, tmc_o + cb:tmc_o + cb + 1], in1=AB[:, vcar_o + cb:vcar_o + cb + 1],
                    op0=ALU.mult, op1=ALU.add), inc=VE)
                P.add("V", lambda v, cb=cb, n=glen128: v.tensor_scalar(
                    out=AB[:, vcar_o + cb:vcar_o + cb + 1], in0=pcbv[cb][:, n - 1:n],
                    scalar1=AFR[:, itm_o + cb:itm_o + cb + 1], scalar2=0.0,
                    op0=ALU.mult, op1=ALU.add), inc=VE)
            tok_evac_g[gi] = (VE, VE.n)

        for r in range(NT + 1):
            if r < NT:
                p1_a(r)
            if 2 <= r < 10:
                wq_chunk(r - 2)
            if r >= 1:
                p1_b(r - 1)
                p1_c(r - 1)
        tok_xm_done = (VE, VE.n)

        P.add("V", lambda v: v.reduce_max(out=AFR[:, amw_o:amw_o + 1],
                                          in_=AFR[:, wqam_o:wqam_o + 8], axis=AX.X),
              waits=[tok_wqam[7]], inc=VE, chain=True)
        tok_amw = (VE, VE.n)
        P.add("G", lambda g: g.reduce_max(out=ONE[0:1, rw_o:rw_o + 1],
                                          in_=AFR[:, amw_o:amw_o + 1], axis=AX.XYZWC),
              waits=[tok_amw], inc=GP)
        tok_rw = (GP, GP.n)
        P.add("V", lambda v: v.tensor_single_scalar(out=ONE[0:1, aw1_o:aw1_o + 1],
                                                    in_=ONE[0:1, rw_o:rw_o + 1], scalar=1e-12,
                                                    op=ALU.max), waits=[tok_rw], inc=VE, chain=True)
        P.add("V", lambda v: v.reciprocal(out=ONE[0:1, rec2_o:rec2_o + 1], in_=ONE[0:1, aw1_o:aw1_o + 1]), inc=VE, chain=True)
        P.add("V", lambda v: v.tensor_single_scalar(out=ONE[0:1, sw1_o:sw1_o + 1],
                                                    in_=ONE[0:1, rec2_o:rec2_o + 1], scalar=448.0,
                                                    op=ALU.mult), inc=VE, chain=True)
        tok_sw1 = (VE, VE.n)
        P.add("T", lambda t: t.matmul(out=PS[:, 3073:3074], lhsT=ONE[0:1, ones_o:ones_o + 128],
                                      rhs=ONE[0:1, sw1_o:sw1_o + 1], start=True, stop=True),
              waits=[tok_sw1, tok_xm_done], inc=TE)
        tok_swbc = (TE, TE.n)
        P.add("V", lambda v: v.tensor_copy(out=AFR[:, swcol_o:swcol_o + 1], in_=PS[:, 3073:3074]),
              waits=[tok_swbc], inc=VE)
        tok_swcol = (VE, VE.n)

        tok_b8q = NOTOK
        tok_a8q = NOTOK
        tok_utq = NOTOK
        tok_qwq = NOTOK
        tok_qv_done = NOTOK

        def wq_quant_head(ch):
            nonlocal tok_b8q, tok_a8q
            prev_qwq = tok_qwq
            tok_c1q = P.add("G", lambda g, ch=ch: g.tensor_scalar(
                out=AFR[:, c1_o:c1_o + 1024], in0=wqslotF[ch],
                scalar1=AFR[:, swcol_o:swcol_o + 1], scalar2=0.0,
                op0=ALU.mult, op1=ALU.add),
                waits=[tok_wql[ch], tok_b8q, tok_swcol if ch == 0 else NOTOK], inc=GP)
            P.add("S", lambda s: s.activation(out=A8[:, b8_o:b8_o + 1024], in_=AFR[:, c1_o:c1_o + 1024],
                                              func=AF.Identity, scale=AFR[:, cst_o + 5:cst_o + 6]),
                  waits=[tok_c1q, prev_qwq], inc=SE)
            tok_b8q = (SE, SE.n)
            tok_ytq = P.add("G", lambda g: g.tensor_scalar(
                out=AFR[:, yt_o:yt_o + 1024], in0=AFR[:, c1_o:c1_o + 1024],
                scalar1=128.0, scalar2=-128.0, op0=ALU.min, op1=ALU.max),
                waits=[tok_a8q], inc=GP, chain=True)
            P.add("S", lambda s: s.activation(out=A8[:, a8_o:a8_o + 1024], in_=AFR[:, yt_o:yt_o + 1024],
                                              func=AF.Identity, scale=AFR[:, cst_o + 2:cst_o + 3]),
                  waits=[tok_ytq, prev_qwq], inc=SE)
            tok_a8q = (SE, SE.n)

        def wq_quant_tail(ch):
            nonlocal tok_qwq, tok_qv_done
            P.add("V", lambda v: v.tensor_scalar(
                out=AB[:, vsb_o:vsb_o + 1024], in0=A8[:, b8_o:b8_o + 1024],
                scalar1=2.0, scalar2=128.0, op0=ALU.mult, op1=ALU.min),
                waits=[tok_b8q], inc=VE)
            P.add("V", lambda v, ch=ch: v.tensor_single_scalar(out=AB[:, qwb + ch * 1024: qwb + (ch + 1) * 1024],
                                                               in_=AB[:, vsb_o:vsb_o + 1024], scalar=-128.0,
                                                               op=ALU.max), inc=VE, chain=True)
            P.add("V", lambda v, ch=ch: v.scalar_tensor_tensor(out=AB[:, vsb_o:vsb_o + 1024],
                                                               in0=A8[:, b8_o:b8_o + 1024], scalar=2.0,
                                                               in1=AB[:, qwb + ch * 1024: qwb + (ch + 1) * 1024],
                                                               op0=ALU.mult, op1=ALU.subtract), inc=VE, chain=True)
            P.add("V", lambda v, ch=ch: v.scalar_tensor_tensor(out=AB[:, qwb + ch * 1024: qwb + (ch + 1) * 1024],
                                                               in0=A8[:, a8_o:a8_o + 1024], scalar=1.0,
                                                               in1=AB[:, vsb_o:vsb_o + 1024], op0=ALU.mult,
                                                               op1=ALU.add),
                  waits=[tok_a8q], inc=VE, chain=True)
            tok_qwq = (VE, VE.n)
            tok_qv_done = tok_qwq

        wsl_bufs = [wslA, wslB]
        stage = [(xa_o, LDA), (xb_o, LDB), (xc_o, LDC)]
        tok_wdma = [[NOTOK] * 3 for _ in range(8)]
        tok_wcast = [NOTOK] * 8
        tok_wcast_wi = [[NOTOK] * 3 for _ in range(8)]
        tok_exp = [NOTOK] * 8
        tok_kv = [NOTOK] * 8
        tok_eneg = [NOTOK] * 8
        tok_kmm = [NOTOK] * 8
        tok_vmm = [NOTOK] * 8
        tok_rmm = [NOTOK] * 8

        def emit_wslice(db):
            wsl = wsl_bufs[db % 2]
            for wi, wd in enumerate((wk_d, wv_d, wr_d)):
                st, ldev = stage[wi]
                if db == 0:
                    wts = [tok_apply[NT - 1]]
                else:
                    wts = [tok_wcast_wi[db - 1][wi]]
                tok_wdma[db][wi] = P.add("Y", lambda y, wd=wd, st=st, db=db: y.dma_start(
                    out=AFR[:, st:st + 1024],
                    in_=wd.rearrange("p (cb d) -> p cb d", cb=8)[:, :, db * 128:(db + 1) * 128]),
                    waits=wts, inc=ldev, k=16)
                P.add("S", lambda s, st=st, wsl=wsl, wi=wi: s.activation(
                    out=AB[:, wsl + wi * 1024: wsl + (wi + 1) * 1024], in_=AFR[:, st:st + 1024],
                    func=AF.Identity),
                    waits=[tok_wdma[db][wi]], inc=SE)
                tok_wcast_wi[db][wi] = (SE, SE.n)
            tok_wcast[db] = (SE, SE.n)

        emit_wslice(0)

        tok_wo_cast = NOTOK

        def emit_wo():
            nonlocal tok_wo_cast
            tok_wo_cast_cb = [NOTOK] * 8
            for cb in range(8):
                st, ldev = (xa_o, LDA) if cb % 2 == 0 else (xb_o, LDB)
                wts = []
                if cb < 2:
                    wts.append(tok_wcast[7])
                if cb >= 2:
                    wts.append(tok_wo_cast_cb[cb - 2])
                tok_wo_dma = P.add("Y", lambda y, cb=cb, st=st: y.dma_start(
                    out=AFR[:, st:st + 1024],
                    in_=wo_d[:, cb * 1024:(cb + 1) * 1024]), waits=wts, inc=ldev, k=16)
                P.add("S", lambda s, cb=cb, st=st: s.activation(
                    out=AB[:, wob + cb * 1024: wob + (cb + 1) * 1024],
                    in_=AFR[:, st:st + 1024], func=AF.Identity),
                    waits=[tok_wo_dma, tok_qv_done if cb == 0 else NOTOK], inc=SE)
                tok_wo_cast_cb[cb] = (SE, SE.n)
            tok_wo_cast = (SE, SE.n)

        tok_sig_prev = NOTOK
        tok_rwkv_prev = NOTOK
        for db in range(8):
            wsl = wsl_bufs[db % 2]
            dcol_ap = AFR[:, dcol_o + db:dcol_o + db + 1]
            tfe_ap = AFR[:, tfe_o + db:tfe_o + db + 1]

            for cb in range(8):
                wts = []
                if cb == 0:
                    wts.append(tok_wcast[db])
                    wts.append(tok_xm_done)
                    if db > 0:
                        wts.append(tok_kv[db - 1])
                lw = wsl + 0 * 1024 + cb * 128
                xmb = xmf + cb * 2176
                for q in range(4):
                    P.add("T", lambda t, lw=lw, xmb=xmb, q=q, cb=cb: t.matmul(
                        out=PS[:, q * 512:(q + 1) * 512], lhsT=AB[:, lw:lw + 128],
                        rhs=AB[:, xmb + 128 + q * 512: xmb + 128 + (q + 1) * 512],
                        start=(cb == 0), stop=(cb == 7)),
                        waits=wts if q == 0 else (), inc=TE)
            tok_kmm_main = (TE, TE.n)
            for cb in range(8):
                wts = []
                if cb == 0 and db > 0:
                    wts.append(tok_eneg[db - 1])
                lw = wsl + 0 * 1024 + cb * 128
                xmb = xmf + cb * 2176
                P.add("T", lambda t, lw=lw, xmb=xmb, cb=cb: t.matmul(
                    out=PS[:, 2048:2176], lhsT=AB[:, lw:lw + 128], rhs=AB[:, xmb:xmb + 128],
                    start=(cb == 0), stop=(cb == 7)), waits=wts, inc=TE)
            tok_kmm[db] = (TE, TE.n)
            if db < 7:
                emit_wslice(db + 1)
            tok_exp_b = []
            for q in range(4):
                P.add("S", lambda s, q=q: s.activation(
                    out=AB[:, pX1 + 128 + q * 512:pX1 + 128 + (q + 1) * 512],
                    in_=PS[:, q * 512:(q + 1) * 512], func=AF.Exp),
                    waits=[tok_kmm_main] if q == 0 else [], inc=SE)
                tok_exp_b.append((SE, SE.n))
            P.add("S", lambda s: s.activation(out=AB[:, pX1:pX1 + 128], in_=PS[:, 2048:2176],
                                              func=AF.Exp), waits=[tok_kmm[db]], inc=SE)
            tok_exp[db] = (SE, SE.n)
            wq_quant_head(db)
            for cb in range(8):
                lw = wsl + 1 * 1024 + cb * 128
                xmb = xmf + cb * 2176
                for q in range(4):
                    P.add("T", lambda t, lw=lw, xmb=xmb, q=q, cb=cb: t.matmul(
                        out=PS[:, q * 512:(q + 1) * 512], lhsT=AB[:, lw:lw + 128],
                        rhs=AB[:, xmb + 128 + q * 512: xmb + 128 + (q + 1) * 512],
                        start=(cb == 0), stop=(cb == 7)),
                        waits=[tok_exp_b[q]] if cb == 0 else (), inc=TE)
            for cb in range(8):
                lw = wsl + 1 * 1024 + cb * 128
                xmb = xmf + cb * 2176
                P.add("T", lambda t, lw=lw, xmb=xmb, cb=cb: t.matmul(
                    out=PS[:, 2176:2304], lhsT=AB[:, lw:lw + 128], rhs=AB[:, xmb:xmb + 128],
                    start=(cb == 0), stop=(cb == 7)), inc=TE)
            tok_vmm[db] = (TE, TE.n)
            P.add("V", lambda v: v.scalar_tensor_tensor(out=AB[:, pX3:pX3 + 128],
                                                        in0=AB[:, pX1:pX1 + 128], scalar=E60,
                                                        in1=PS[:, 2176:2304],
                                                        op0=ALU.min, op1=ALU.mult),
                  waits=[tok_vmm[db], tok_exp[db], tok_rwkv_prev], inc=VE)
            tok_kv_warm = (VE, VE.n)
            P.add("V", lambda v: v.scalar_tensor_tensor(out=AB[:, pX3 + 128:pX3 + 2176],
                                                        in0=AB[:, pX1 + 128:pX1 + 2176], scalar=E60,
                                                        in1=PS[:, 0:2048],
                                                        op0=ALU.min, op1=ALU.mult), inc=VE)
            tok_kv[db] = (VE, VE.n)
            tok_min = P.add("V", lambda v: v.tensor_single_scalar(
                out=AB[:, pX2:pX2 + 2176], in_=AB[:, pX1:pX1 + 2176], scalar=E60,
                op=ALU.min),
                waits=[tok_sig_prev], inc=VE)
            for (src, dsty) in ((pX3, pY1), (pX2, pY2)):
                P.add("V", lambda v, d=dcol_ap, src=src: v.tensor_tensor_scan(
                    out=AFR[:, bw_o:bw_o + 128], data0=d.broadcast_to([128, 128]),
                    data1=AB[:, src:src + 128], initial=0.0, op0=ALU.mult, op1=ALU.add), inc=VE)
                P.add("V", lambda v: v.tensor_scalar(out=AFR[:, bbuf_o:bbuf_o + 1],
                                                     in0=AFR[:, bw_o + 127:bw_o + 128],
                                                     scalar1=AFR[:, maskc_o:maskc_o + 1], scalar2=0.0,
                                                     op0=ALU.mult, op1=ALU.add), inc=VE, chain=True)
                P.add("V", lambda v, d=dcol_ap, src=src: v.tensor_tensor_scan(
                    out=AFR[:, bbuf_o + 1:bbuf_o + 2049], data0=d.broadcast_to([128, 2048]),
                    data1=AB[:, src + 128:src + 2176], initial=AFR[:, bbuf_o:bbuf_o + 1],
                    op0=ALU.mult, op1=ALU.add), inc=VE, chain=True)
                P.add("V", lambda v, tf=tfe_ap, src=src, dsty=dsty: v.scalar_tensor_tensor(
                    out=AB[:, dsty:dsty + 2048], in0=AB[:, src + 128:src + 2176], scalar=tf,
                    in1=AFR[:, bbuf_o:bbuf_o + 2048], op0=ALU.mult, op1=ALU.add), inc=VE)
            tok_wq_v = (VE, VE.n)
            for cb in range(8):
                wts = []
                if cb == 0:
                    wts.append(tok_kv_warm)
                    if db > 0:
                        wts.append(tok_eneg[db - 1])
                    else:
                        wts.append(tok_swcol)
                lw = wsl + 2 * 1024 + cb * 128
                xmb = xmf + cb * 2176
                for q in range(4):
                    P.add("T", lambda t, lw=lw, xmb=xmb, q=q, cb=cb: t.matmul(
                        out=PS[:, 2048 + q * 512:2048 + (q + 1) * 512], lhsT=AB[:, lw:lw + 128],
                        rhs=AB[:, xmb + 128 + q * 512: xmb + 128 + (q + 1) * 512],
                        start=(cb == 0), stop=(cb == 7)),
                        waits=wts if q == 0 else (), inc=TE)
            tok_rmm[db] = (TE, TE.n)
            P.add("S", lambda s: s.activation(out=AB[:, pX1:pX1 + 2048], in_=PS[:, 2048:4096],
                                              func=AF.Exp, scale=AFR[:, cst_o + 3:cst_o + 4]),
                  waits=[tok_rmm[db], tok_min], inc=SE)
            tok_eneg[db] = (SE, SE.n)
            P.add("S", lambda s: s.activation(out=AB[:, pY3:pY3 + 2048], in_=AB[:, pY2:pY2 + 2048],
                                              func=AF.Ln, bias=AFR[:, cst_o + 1:cst_o + 2]),
                  waits=[tok_wq_v, tok_rwkv_prev], inc=SE)
            P.add("S", lambda s: s.activation(out=AB[:, pY2:pY2 + 2048], in_=AB[:, pY3:pY3 + 2048],
                                              func=AF.Exp, scale=AFR[:, cst_o + 3:cst_o + 4]), inc=SE)
            tok_rcp = (SE, SE.n)
            P.add("G", lambda g: g.tensor_tensor(out=AB[:, pY3:pY3 + 2048], in0=AB[:, pY1:pY1 + 2048],
                                                 in1=AB[:, pY2:pY2 + 2048], op=ALU.mult),
                  waits=[tok_rcp], inc=GP)
            tok_ratio = (GP, GP.n)
            P.add("S", lambda s: s.activation(out=AB[:, pX2:pX2 + 2048], in_=AB[:, pX1:pX1 + 2048],
                                              func=AF.Ln, bias=AFR[:, cst_o + 2:cst_o + 3]),
                  waits=[tok_wq_v], inc=SE)
            P.add("S", lambda s: s.activation(out=AB[:, pX3:pX3 + 2048], in_=AB[:, pX2:pX2 + 2048],
                                              func=AF.Exp, scale=AFR[:, cst_o + 3:cst_o + 4]), inc=SE)
            tok_sig = (SE, SE.n)
            P.add("G", lambda g, db=db: g.tensor_tensor(
                out=AB[:, rwkb + db * 2048: rwkb + (db + 1) * 2048],
                in0=AB[:, pX3:pX3 + 2048], in1=AB[:, pY3:pY3 + 2048], op=ALU.mult),
                waits=[tok_sig], inc=GP)
            tok_rwkv_prev = (GP, GP.n)
            tok_sig_prev = tok_sig
            wq_quant_tail(db)
            if db == 7:
                emit_wo()
        tok_p2_v = tok_rwkv_prev
        tok_p2_s = (SE, SE.n)

        ln2tp = [ln2t_o, c1_o]
        yt3p = [yt_o, et_o]
        tok_xrel = [NOTOK] * NTM
        tok_womm = [NOTOK] * NTM
        tok_bn2v = [NOTOK] * NTM
        tok_rstd2 = [NOTOK] * NTM
        tok_nmr2v = [NOTOK] * NTM
        tok_ln2t = [NOTOK] * NTM
        tok_x1 = [NOTOK] * NTM
        tok_x1bf = [NOTOK] * NTM
        tok_rstd3 = [NOTOK] * NTM
        tok_p3v = [NOTOK] * NTM
        tok_amax = [NOTOK] * NTM

        def p3_stage_a(tc):
            p = tc % 2
            pb = p * 1024
            xro, ldev = (xa_o, LDA) if p == 0 else (xb_o, LDB)
            wts = [tok_wo_cast] if tc < 2 else [tok_x1[tc - 2]]
            tok_xrel[tc] = P.add("Y", lambda y, tc=tc, xro=xro: y.dma_start(
                out=AFR[:, xro:xro + 1024], in_=x_d[W + tc * 128: W + (tc + 1) * 128, :]),
                waits=wts, inc=ldev, k=16)
            for cb in range(8):
                wts = []
                if cb == 0:
                    wts.append(tok_wo_cast)
                    if tc == 0:
                        wts.append(tok_p2_v)
                    if tc >= 2:
                        wts.append(tok_ln2t[tc - 2])
                        wts.append(tok_bn2v[tc - 2])
                ro = rwkb + cb * 2048 + tc * 128
                P.add("T", lambda t, ro=ro, cb=cb, pb=pb: t.matmul(
                    out=PS[:, pb:pb + 512], lhsT=AB[:, ro:ro + 128],
                    rhs=AB[:, wob + cb * 1024: wob + cb * 1024 + 512],
                    start=(cb == 0), stop=(cb == 7)), waits=wts, inc=TE)
                P.add("T", lambda t, ro=ro, cb=cb, pb=pb: t.matmul(
                    out=PS[:, pb + 512:pb + 1024], lhsT=AB[:, ro:ro + 128],
                    rhs=AB[:, wob + cb * 1024 + 512: wob + (cb + 1) * 1024],
                    start=(cb == 0), stop=(cb == 7)), inc=TE)
            tok_womm[tc] = (TE, TE.n)
            P.add("V", lambda v, pb=pb, p=p: v.bn_stats(out=AFR[:, bnst2_o[p]:bnst2_o[p] + 6],
                                                        in_=PS[:, pb:pb + 512]),
                  waits=[tok_womm[tc]], inc=VE)
            P.add("V", lambda v, pb=pb, p=p: v.bn_stats(out=AFR[:, bnst2_o[p] + 6:bnst2_o[p] + 12],
                                                        in_=PS[:, pb + 512:pb + 1024]), inc=VE)
            P.add("V", lambda v, p=p: v.bn_aggr(out=AFR[:, mv2s_o[p]:mv2s_o[p] + 2],
                                                in_=AFR[:, bnst2_o[p]:bnst2_o[p] + 12]),
                  waits=[tok_rstd2[tc - 2]] if tc >= 2 else [], inc=VE, chain=True)
            tok_bn2v[tc] = (VE, VE.n)
            P.add("S", lambda s, p=p: s.activation(out=AFR[:, u2_o[p]:u2_o[p] + 1],
                                                   in_=AFR[:, mv2s_o[p] + 1:mv2s_o[p] + 2],
                                                   func=AF.Ln, bias=AFR[:, cst_o:cst_o + 1]),
                  waits=[tok_bn2v[tc], tok_nmr2v[tc - 2] if tc >= 2 else NOTOK], inc=SE, chain=True)
            P.add("S", lambda s, p=p: s.activation(out=AFR[:, rc2_o[p]:rc2_o[p] + 1],
                                                   in_=AFR[:, u2_o[p]:u2_o[p] + 1], func=AF.Exp,
                                                   scale=AFR[:, cst_o + 4:cst_o + 5]), inc=SE, chain=True)
            tok_rstd2[tc] = (SE, SE.n)

        def p3_stage_b(tc):
            p = tc % 2
            pb = p * 1024
            xro = xa_o if p == 0 else xb_o
            x1o = x1ta_o if p == 0 else x1tb_o
            P.add("V", lambda v, p=p: v.tensor_scalar(out=AFR[:, nm2_o[p]:nm2_o[p] + 1],
                                                      in0=AFR[:, mv2s_o[p]:mv2s_o[p] + 1],
                                                      scalar1=AFR[:, rc2_o[p]:rc2_o[p] + 1], scalar2=-1.0,
                                                      op0=ALU.mult, op1=ALU.mult),
                  waits=[tok_rstd2[tc], tok_ln2t[tc - 2] if tc >= 2 else NOTOK], inc=VE, chain=True)
            tok_nmr2v[tc] = (VE, VE.n)
            P.add("S", lambda s, pb=pb, p=p: s.activation(out=AFR[:, ln2tp[p]:ln2tp[p] + 1024],
                                                          in_=PS[:, pb:pb + 1024],
                                                          func=AF.Identity,
                                                          bias=AFR[:, nm2_o[p]:nm2_o[p] + 1],
                                                          scale=AFR[:, rc2_o[p]:rc2_o[p] + 1]),
                  waits=[tok_nmr2v[tc], tok_x1[tc - 2] if tc >= 2 else NOTOK], inc=SE, chain=True)
            tok_ln2t[tc] = (SE, SE.n)
            tok_x1[tc] = P.add("G", lambda g, xro=xro, x1o=x1o, p=p: g.tensor_tensor(
                out=AFR[:, x1o:x1o + 1024], in0=AFR[:, xro:xro + 1024],
                in1=AFR[:, ln2tp[p]:ln2tp[p] + 1024], op=ALU.add),
                waits=[tok_ln2t[tc], tok_xrel[tc],
                       tok_p3v[tc - 2] if tc >= 2 else NOTOK,
                       tok_x1bf[tc - 2] if tc >= 2 else NOTOK], inc=GP)

        def p3_stage_c(tc):
            p = tc % 2
            x1o = x1ta_o if p == 0 else x1tb_o
            P.add("S", lambda s, tc=tc, x1o=x1o: s.activation(
                out=AB[:, x1bf_o + tc * 1024: x1bf_o + (tc + 1) * 1024],
                in_=AFR[:, x1o:x1o + 1024], func=AF.Identity),
                waits=[tok_x1[tc]], inc=SE)
            tok_x1bf[tc] = (SE, SE.n)
            P.add("V", lambda v, x1o=x1o, p=p: v.bn_stats(out=AFR[:, bnst3_o[p]:bnst3_o[p] + 6],
                                                          in_=AFR[:, x1o:x1o + 512]),
                  waits=[tok_x1[tc]], inc=VE)
            P.add("V", lambda v, x1o=x1o, p=p: v.bn_stats(out=AFR[:, bnst3_o[p] + 6:bnst3_o[p] + 12],
                                                          in_=AFR[:, x1o + 512:x1o + 1024]), inc=VE)
            P.add("V", lambda v, tc=tc, p=p: v.bn_aggr(out=AFR[:, mv3_o + 2 * tc:mv3_o + 2 * tc + 2],
                                                       in_=AFR[:, bnst3_o[p]:bnst3_o[p] + 12]),
                  inc=VE, chain=True)
            tok_bn3v = (VE, VE.n)
            P.add("S", lambda s, tc=tc, p=p: s.activation(out=AFR[:, u3_o[p]:u3_o[p] + 1],
                                                          in_=AFR[:, mv3_o + 2 * tc + 1:mv3_o + 2 * tc + 2],
                                                          func=AF.Ln, bias=AFR[:, cst_o:cst_o + 1]),
                  waits=[tok_bn3v], inc=SE, chain=True)
            P.add("S", lambda s, tc=tc, p=p: s.activation(out=AFR[:, r3a_o + tc:r3a_o + tc + 1],
                                                          in_=AFR[:, u3_o[p]:u3_o[p] + 1], func=AF.Exp,
                                                          scale=AFR[:, cst_o + 4:cst_o + 5]),
                  inc=SE, chain=True)
            tok_rstd3[tc] = (SE, SE.n)

        def p3_stage_d(tc):
            p = tc % 2
            x1o = x1ta_o if p == 0 else x1tb_o
            P.add("V", lambda v, tc=tc: v.tensor_scalar(out=AFR[:, nm3p_o + tc:nm3p_o + tc + 1],
                                                        in0=AFR[:, mv3_o + 2 * tc:mv3_o + 2 * tc + 1],
                                                        scalar1=AFR[:, r3a_o + tc:r3a_o + tc + 1],
                                                        scalar2=-1.0,
                                                        op0=ALU.mult, op1=ALU.mult),
                  waits=[tok_rstd3[tc]], inc=VE, chain=True)
            P.add("V", lambda v, tc=tc, x1o=x1o, p=p: v.tensor_scalar(
                out=AFR[:, yt3p[p]:yt3p[p] + 1024], in0=AFR[:, x1o:x1o + 1024],
                scalar1=AFR[:, r3a_o + tc:r3a_o + tc + 1],
                scalar2=AFR[:, nm3p_o + tc:nm3p_o + tc + 1],
                op0=ALU.mult, op1=ALU.add),
                waits=[tok_amax[tc - 2] if tc >= 2 else NOTOK], inc=VE, chain=True)
            tok_p3v[tc] = (VE, VE.n)
            tok_amax[tc] = P.add("G", lambda g, tc=tc, p=p: g.reduce_max(
                out=ONE[0:1, amsc_o + tc:amsc_o + tc + 1], in_=AFR[:, yt3p[p]:yt3p[p] + 1024],
                axis=AX.XYZWC, apply_absolute_value=True),
                waits=[tok_p3v[tc]], inc=GP)

        for r in range(NTM + 1):
            if r < NTM:
                p3_stage_a(r)
            if r >= 1:
                p3_stage_c(r - 1)
            if r < NTM:
                p3_stage_b(r)
            if r >= 1:
                p3_stage_d(r - 1)

        P.add("G", lambda g: g.reduce_max(out=ONE[0:1, ccsb_o:ccsb_o + 1],
                                          in_=ONE[0:1, amsc_o:amsc_o + 16], axis=AX.XYZWC),
              inc=GP, chain=True)
        tok_ccsb = (GP, GP.n)
        tok_cci = P.add("G", lambda g: g.dma_start(out=cci[:, :], in_=ONE[0:1, ccsb_o:ccsb_o + 16]),
                        waits=[tok_ccsb, (CC, 1)], inc=CCD, k=16)
        P.add("G", lambda g: g.collective_compute(
            "AllReduce", ALU.max, replica_groups=[list(range(8))],
            ins=[cci.ap().opt()], outs=[cco.ap().opt()]),
            waits=[tok_cci], inc=CC)

        P.add("G", lambda g: g.dma_start(out=ONE[0:1, ccs_o:ccs_o + 16], in_=cco.ap()),
              waits=[(CC, 2)], inc=CCD, k=16)
        tok_ccback = (CCD, CCD.n)
        P.add("V", lambda v: v.tensor_single_scalar(out=ONE[0:1, am1_o:am1_o + 1],
                                                    in_=ONE[0:1, ccs_o:ccs_o + 1], scalar=1e-12,
                                                    op=ALU.max), waits=[tok_ccback], inc=VE, chain=True)
        P.add("V", lambda v: v.reciprocal(out=ONE[0:1, rec1_o:rec1_o + 1], in_=ONE[0:1, am1_o:am1_o + 1]), inc=VE, chain=True)
        P.add("V", lambda v: v.tensor_single_scalar(out=ONE[0:1, s1_o:s1_o + 1],
                                                    in_=ONE[0:1, rec1_o:rec1_o + 1], scalar=448.0,
                                                    op=ALU.mult), inc=VE, chain=True)
        P.add("V", lambda v: v.tensor_tensor(out=ONE[0:1, prod_o:prod_o + 1], in0=ONE[0:1, s1_o:s1_o + 1],
                                             in1=ONE[0:1, sw1_o:sw1_o + 1], op=ALU.mult), inc=VE, chain=True)
        P.add("V", lambda v: v.reciprocal(out=ONE[0:1, inv1_o:inv1_o + 1], in_=ONE[0:1, prod_o:prod_o + 1]), inc=VE, chain=True)
        tok_scalars = (VE, VE.n)
        P.add("T", lambda t: t.matmul(out=PS[:, 3072:3073], lhsT=ONE[0:1, ones_o:ones_o + 128],
                                      rhs=ONE[0:1, s1_o:s1_o + 1], start=True, stop=True),
              waits=[tok_scalars], inc=TE)
        P.add("T", lambda t: t.matmul(out=PS[:, 3074:3075], lhsT=ONE[0:1, ones_o:ones_o + 128],
                                      rhs=ONE[0:1, inv1_o:inv1_o + 1], start=True, stop=True), inc=TE)
        tok_bc = (TE, TE.n)
        P.add("V", lambda v: v.tensor_copy(out=AFR[:, scol_o:scol_o + 1], in_=PS[:, 3072:3073]),
              waits=[tok_bc], inc=VE)
        P.add("V", lambda v: v.tensor_copy(out=AFR[:, invcol_o:invcol_o + 1], in_=PS[:, 3074:3075]), inc=VE)
        P.add("V", lambda v: v.tensor_scalar(out=AFR[:, r3sa_o:r3sa_o + 16], in0=AFR[:, r3a_o:r3a_o + 16],
                                             scalar1=AFR[:, scol_o:scol_o + 1], scalar2=0.0,
                                             op0=ALU.mult, op1=ALU.add), inc=VE, chain=True)
        P.add("V", lambda v: v.tensor_scalar(out=AFR[:, nm3a_o:nm3a_o + 16], in0=AFR[:, nm3p_o:nm3p_o + 16],
                                             scalar1=AFR[:, scol_o:scol_o + 1], scalar2=0.0,
                                             op0=ALU.mult, op1=ALU.add), inc=VE, chain=True)
        tok_qscale = (VE, VE.n)

        ytp5 = [yt_o, xc_o]
        c1p5 = [c1_o, ln2t_o]
        etp5 = [et_o, x1ta_o]
        t4p5 = [t4t_o, x1tb_o]
        b8ap = [A8[:, b8_o:b8_o + 1024], AB[:, wslA:wslA + 512].bitcast(FP8)]
        a8ap = [A8[:, a8_o:a8_o + 1024], AB[:, wslA + 512:wslA + 1024].bitcast(FP8)]
        utp = [ut_o, pX1]; wt2p = [wt2_o, pX1 + 1024]
        t2tp = [t2t_o, pX2]; qtp = [qt_o, pX2 + 1024]
        xqfp = [xqf_o, pX3]
        pqvp = [PS[:, 2048:2560].bitcast(BF16), PS[:, 2560:3072].bitcast(BF16)]
        bnst4_o = [af(12), af(12)]; mv4s_o = [af(2), af(2)]
        u4_o = [af(1), af(1)]; rc4_o = [af(1), af(1)]; nm4_o = [af(1), af(1)]
        assert o2 <= 13200, o2

        tok_out = [NOTOK] * NTM
        tok_outv = [NOTOK] * NTM
        tok_etv = [NOTOK] * NTM
        tok_t4 = [NOTOK] * NTM
        tok_xqfS = [NOTOK] * NTM
        tok_ffn = [NOTOK] * NTM
        tok_trq = [NOTOK] * NTM
        tok_qtv = [NOTOK] * NTM
        tok_b8x = [NOTOK] * NTM
        tok_a8v = [NOTOK] * NTM
        tok_rstd4 = [NOTOK] * NTM
        tok_nmr4v = [NOTOK] * NTM

        def p5_q1(tc):
            p = tc % 2
            x1c = x1bf_o + tc * 1024
            tok_yt = P.add("G", lambda g, tc=tc, x1c=x1c, p=p: g.tensor_scalar(
                out=AFR[:, ytp5[p]:ytp5[p] + 1024], in0=AB[:, x1c:x1c + 1024],
                scalar1=AFR[:, r3sa_o + tc:r3sa_o + tc + 1],
                scalar2=AFR[:, nm3a_o + tc:nm3a_o + tc + 1], op0=ALU.mult, op1=ALU.add),
                waits=[tok_qscale if tc == 0 else NOTOK,
                       tok_b8x[tc - 2] if tc >= 2 else NOTOK], inc=GP)
            P.add("S", lambda s, p=p: s.activation(out=b8ap[p], in_=AFR[:, ytp5[p]:ytp5[p] + 1024],
                                                   func=AF.Identity, scale=AFR[:, cst_o + 5:cst_o + 6]),
                  waits=[tok_yt, tok_qtv[tc - 2] if tc >= 2 else NOTOK], inc=SE)
            tok_b8x[tc] = (SE, SE.n)
            tok_c1 = P.add("G", lambda g, p=p: g.tensor_scalar(
                out=AFR[:, c1p5[p]:c1p5[p] + 1024], in0=AFR[:, ytp5[p]:ytp5[p] + 1024],
                scalar1=128.0, scalar2=-128.0, op0=ALU.min, op1=ALU.max),
                waits=[tok_a8v[tc - 2] if tc >= 2 else NOTOK], inc=GP, chain=True)
            P.add("V", lambda v, p=p: v.tensor_copy(out=a8ap[p], in_=AFR[:, c1p5[p]:c1p5[p] + 1024]),
                  waits=[tok_c1], inc=VE)
            tok_a8v[tc] = (VE, VE.n)
            P.add("V", lambda v, p=p: v.tensor_scalar(out=AB[:, utp[p]:utp[p] + 1024], in0=b8ap[p],
                                                      scalar1=2.0, scalar2=128.0,
                                                      op0=ALU.mult, op1=ALU.min),
                  waits=[tok_b8x[tc]], inc=VE)
            P.add("V", lambda v, p=p: v.tensor_single_scalar(out=AB[:, wt2p[p]:wt2p[p] + 1024],
                                                             in_=AB[:, utp[p]:utp[p] + 1024],
                                                             scalar=-128.0, op=ALU.max), inc=VE)
            P.add("V", lambda v, p=p: v.scalar_tensor_tensor(out=AB[:, t2tp[p]:t2tp[p] + 1024],
                                                             in0=b8ap[p], scalar=2.0,
                                                             in1=AB[:, wt2p[p]:wt2p[p] + 1024],
                                                             op0=ALU.mult, op1=ALU.subtract), inc=VE)
            P.add("V", lambda v, p=p: v.scalar_tensor_tensor(out=AB[:, qtp[p]:qtp[p] + 1024],
                                                             in0=a8ap[p], scalar=1.0,
                                                             in1=AB[:, t2tp[p]:t2tp[p] + 1024],
                                                             op0=ALU.mult, op1=ALU.add),
                  waits=[tok_trq[tc - 2] if tc >= 2 else NOTOK], inc=VE)
            tok_qtv[tc] = (VE, VE.n)

        def p5_q2(tc):
            p = tc % 2
            for cb in range(8):
                wts = [tok_qtv[tc]] if cb == 0 else []
                if cb == 0 and tc >= 2:
                    wts.append(tok_xqfS[tc - 2])
                P.add("T", lambda t, cb=cb, p=p: t.transpose(
                    out=pqvp[p][:, cb * 128:(cb + 1) * 128],
                    in_=AB[:, qtp[p] + cb * 128: qtp[p] + (cb + 1) * 128],
                    identity=AB[:, idb_o:idb_o + 128]), waits=wts, inc=TE)
            tok_trq[tc] = (TE, TE.n)
            P.add("S", lambda s, p=p: s.activation(out=AB[:, xqfp[p]:xqfp[p] + 1024],
                                                   in_=pqvp[p][:, :], func=AF.Identity),
                  waits=[tok_trq[tc], tok_ffn[tc - 2] if tc >= 2 else NOTOK], inc=SE)
            tok_xqfS[tc] = (SE, SE.n)

        def p5_mm(tc):
            p = tc % 2
            pb = p * 1024
            for cb in range(8):
                wts = []
                if cb == 0:
                    wts = [tok_xqfS[tc]]
                    if tc >= 2:
                        wts.append(tok_etv[tc - 2])
                P.add("T", lambda t, cb=cb, pb=pb, p=p: t.matmul(
                    out=PS[:, pb:pb + 512], lhsT=AB[:, xqfp[p] + cb * 128: xqfp[p] + (cb + 1) * 128],
                    rhs=AB[:, qwb + cb * 1024: qwb + cb * 1024 + 512],
                    start=(cb == 0), stop=(cb == 7)), waits=wts, inc=TE)
                P.add("T", lambda t, cb=cb, pb=pb, p=p: t.matmul(
                    out=PS[:, pb + 512:pb + 1024], lhsT=AB[:, xqfp[p] + cb * 128: xqfp[p] + (cb + 1) * 128],
                    rhs=AB[:, qwb + cb * 1024 + 512: qwb + (cb + 1) * 1024],
                    start=(cb == 0), stop=(cb == 7)), inc=TE)
            tok_ffn[tc] = (TE, TE.n)

        def p5_ln(tc):
            p = tc % 2
            pb = p * 1024
            P.add("V", lambda v, pb=pb, p=p: v.scalar_tensor_tensor(
                out=AFR[:, etp5[p]:etp5[p] + 1024], in0=PS[:, pb:pb + 1024],
                scalar=AFR[:, invcol_o:invcol_o + 1], in1=AFR[:, biasb_o:biasb_o + 1024],
                op0=ALU.mult, op1=ALU.add),
                waits=[tok_ffn[tc], tok_t4[tc - 2] if tc >= 2 else NOTOK], inc=VE)
            tok_etv[tc] = (VE, VE.n)
            P.add("V", lambda v, p=p: v.bn_stats(out=AFR[:, bnst4_o[p]:bnst4_o[p] + 6],
                                                 in_=AFR[:, etp5[p]:etp5[p] + 512]), inc=VE)
            P.add("V", lambda v, p=p: v.bn_stats(out=AFR[:, bnst4_o[p] + 6:bnst4_o[p] + 12],
                                                 in_=AFR[:, etp5[p] + 512:etp5[p] + 1024]), inc=VE)
            P.add("V", lambda v, p=p: v.bn_aggr(out=AFR[:, mv4s_o[p]:mv4s_o[p] + 2],
                                                in_=AFR[:, bnst4_o[p]:bnst4_o[p] + 12]),
                  waits=[tok_rstd4[tc - 2]] if tc >= 2 else [], inc=VE, chain=True)
            tok_bn4v = (VE, VE.n)
            P.add("S", lambda s, p=p: s.activation(out=AFR[:, u4_o[p]:u4_o[p] + 1],
                                                   in_=AFR[:, mv4s_o[p] + 1:mv4s_o[p] + 2],
                                                   func=AF.Ln, bias=AFR[:, cst_o:cst_o + 1]),
                  waits=[tok_bn4v, tok_nmr4v[tc - 2] if tc >= 2 else NOTOK], inc=SE, chain=True)
            P.add("S", lambda s, p=p: s.activation(out=AFR[:, rc4_o[p]:rc4_o[p] + 1],
                                                   in_=AFR[:, u4_o[p]:u4_o[p] + 1], func=AF.Exp,
                                                   scale=AFR[:, cst_o + 4:cst_o + 5]), inc=SE, chain=True)
            tok_rstd4[tc] = (SE, SE.n)
            P.add("V", lambda v, p=p: v.tensor_scalar(out=AFR[:, nm4_o[p]:nm4_o[p] + 1],
                                                      in0=AFR[:, mv4s_o[p]:mv4s_o[p] + 1],
                                                      scalar1=AFR[:, rc4_o[p]:rc4_o[p] + 1], scalar2=-1.0,
                                                      op0=ALU.mult, op1=ALU.mult),
                  waits=[tok_rstd4[tc], tok_t4[tc - 2] if tc >= 2 else NOTOK], inc=VE, chain=True)
            tok_nmr4v[tc] = (VE, VE.n)

        def p5_tail_s(tc):
            p = tc % 2
            P.add("S", lambda s, p=p: s.activation(out=AFR[:, t4p5[p]:t4p5[p] + 1024],
                                                   in_=AFR[:, etp5[p]:etp5[p] + 1024],
                                                   func=AF.Identity, bias=AFR[:, nm4_o[p]:nm4_o[p] + 1],
                                                   scale=AFR[:, rc4_o[p]:rc4_o[p] + 1]),
                  waits=[tok_nmr4v[tc], tok_outv[tc - 2] if tc >= 2 else NOTOK], inc=SE, chain=True)
            tok_t4[tc] = (SE, SE.n)

        def p5_out(tc):
            p = tc % 2
            x1c = x1bf_o + tc * 1024
            oo = outa_o if p == 0 else outb_o
            go = GOA if p == 0 else GOB
            tok_outv[tc] = P.add("G", lambda g, x1c=x1c, oo=oo, p=p: g.tensor_tensor(
                out=AFR[:, oo:oo + 1024], in0=AB[:, x1c:x1c + 1024],
                in1=AFR[:, t4p5[p]:t4p5[p] + 1024], op=ALU.add),
                waits=[tok_t4[tc], tok_out[tc - 2] if tc >= 2 else NOTOK], inc=GP)
            tok_out[tc] = P.add("G", lambda g, tc=tc, oo=oo: g.dma_start(
                out=out_d[tc * 128:(tc + 1) * 128, :], in_=AFR[:, oo:oo + 1024]),
                waits=[tok_outv[tc]], inc=go, k=16)

        for r in range(NTM + 3):
            if 0 <= r - 3:
                p5_tail_s(r - 3)
            if r < NTM:
                p5_q1(r)
            if 0 <= r - 1 < NTM:
                p5_mm(r - 1)
            if 0 <= r - 2 < NTM:
                p5_ln(r - 2)
            if r < NTM:
                p5_q2(r)
            if 0 <= r - 3:
                p5_out(r - 3)
        goa_n, gob_n = GOA.n, GOB.n
        P.add("G", lambda g: (g.wait_ge(GOA.sem, goa_n), g.wait_ge(GOB.sem, gob_n)) and None)

        with nc.Block() as block:
            @block.sync
            def _(y):
                P.emit("Y", y)

            @block.gpsimd
            def _(g):
                P.emit("G", g)

            @block.vector
            def _(v):
                P.emit("V", v)

            @block.scalar
            def _(s):
                P.emit("S", s)

            @block.tensor
            def _(t):
                P.emit("T", t)

    return nc


def _prep_inputs(**inputs):
    x = np.asarray(inputs["L0_attn_x"], np.float32)
    td = np.asarray(inputs["L0_attn_time_decay"], np.float32)[:, 0]
    tf = np.asarray(inputs["L0_attn_time_first"], np.float32)[:, 0]
    tm = np.asarray(inputs["L0_attn_time_mix"], np.float32).reshape(1024)
    Wk = np.asarray(inputs["L0_attn_Wk"], np.float32)
    Wv = np.asarray(inputs["L0_attn_Wv"], np.float32)
    Wr = np.asarray(inputs["L0_attn_Wr"], np.float32)
    Wo = np.asarray(inputs["L0_attn_Wo"], np.float32)
    Wq = np.asarray(inputs["L0_ffn_weight"], np.float32)
    bias = np.asarray(inputs["L0_ffn_bias"], np.float32)

    def swz(Wm):
        return np.ascontiguousarray(
            Wm.T.reshape(8, 128, 1024).transpose(1, 0, 2).reshape(128, 8192))

    def col8(v):
        return np.ascontiguousarray(v.reshape(8, 128).T)

    shared = dict(
        wk=swz(Wk), wv=swz(Wv), wr=swz(Wr), wo=swz(Wo), wq=swz(Wq),
        tm=col8(tm), td=col8(td), tf=col8(tf),
        biasb=np.ascontiguousarray(np.broadcast_to(bias, (128, 1024))),
        idb=np.eye(128, dtype=ml_dtypes.bfloat16),
        idf=np.eye(128, dtype=np.float32),
        ones=np.ones((1, 128), np.float32),
    )
    in_maps = []
    for core in range(8):
        b, h = core // 2, core % 2
        t0 = h * T
        if h == 0:
            warm = np.zeros((W, 1024), np.float32)
        else:
            warm = x[b, t0 - W:t0]
        xc = np.ascontiguousarray(np.concatenate([warm, x[b, t0:t0 + T]], axis=0))
        m = dict(shared)
        m["x"] = xc
        m["mask"] = np.full((128, 1), float(h), np.float32)
        in_maps.append(m)
    return in_maps


def kernel(**inputs):
    in_maps = _prep_inputs(**inputs)
    nc = build_nc()
    res = run_bass_kernel_spmd(nc, in_maps, core_ids=list(range(8)))
    out = np.empty((4, 4096, 1024), np.float32)
    for core in range(8):
        b, h = core // 2, core % 2
        out[b, h * T:(h + 1) * T] = res.results[core]["out"]
    return out
